# version 09
# baseline (speedup 1.0000x reference)
import os
import numpy as np
import ml_dtypes
from contextlib import ExitStack
import concourse.bass as bass
import concourse.mybir as mybir
from concourse.bass_utils import run_bass_kernel_spmd

F32 = mybir.dt.float32
BF16 = mybir.dt.bfloat16
AF = mybir.ActivationFunctionType
ALU = mybir.AluOpType
AX = mybir.AxisListType

NCORE = 8
D = 1024
S = 4096
T = 32768
NTILE = 32
NCOL = 256
E = 8
F4 = 4096
CAP = 9831
W = 64
OV = 128
SLOT_E = NTILE * W + OV
NSC = SLOT_E // 128
NH = 16
HD = 64
BIG = 1.0e9
GELU_C = 0.044715
GELU_S = 1.5957691216057308

ENGINES = ("sync", "gpsimd", "vector", "scalar", "tensor")


class Prog:
    def __init__(self):
        self.ops = {e: [] for e in ENGINES}
        self.cnt = {}

    def add(self, eng, fn, waits=(), inc=None):
        waits = tuple((s, v) for (s, v) in waits if v > 0)
        self.ops[eng].append((fn, waits, inc))
        if inc is not None:
            s, n = inc
            self.cnt[s] = self.cnt.get(s, 0) + n
            return self.cnt[s]
        return None

    def val(self, s):
        return self.cnt.get(s, 0)

    def emit(self, block):
        for ename in ENGINES:
            lst = self.ops[ename]
            if not lst:
                continue

            def body(eng, lst=lst):
                for fn, waits, inc in lst:
                    for (s, v) in waits:
                        eng.wait_ge(s, v)
                    ins = fn(eng)
                    if inc is not None:
                        ins.then_inc(inc[0], inc[1])

            getattr(block, ename)(body)


def build(debug=0):
    nc = bass.Bass("TRN2", target_bir_lowering=False)
    dt = nc.dram_tensor

    x_sh = dt("x_sh", [S, D], F32, kind="ExternalInput").ap()
    xfm = dt("xfm", [D, T], F32, kind="ExternalInput").ap()
    wq = dt("wq", [D, D], F32, kind="ExternalInput").ap()
    wk = dt("wk", [D, D], F32, kind="ExternalInput").ap()
    wv = dt("wv", [D, D], F32, kind="ExternalInput").ap()
    wo = dt("wo", [D, D], F32, kind="ExternalInput").ap()
    bqr = dt("bqr", [128, 8], F32, kind="ExternalInput").ap()
    bkr = dt("bkr", [128, 8], F32, kind="ExternalInput").ap()
    bvrow = dt("bvrow", [1, D], F32, kind="ExternalInput").ap()
    borow = dt("borow", [1, D], F32, kind="ExternalInput").ap()
    wgi = dt("wgi", [D, E], F32, kind="ExternalInput").ap()
    bgc = dt("bgc", [8, 1], F32, kind="ExternalInput").ap()
    w1 = dt("w1", [E, D, F4], F32, kind="ExternalInput").ap()
    b1r = dt("b1r", [128, 32, 8], F32, kind="ExternalInput").ap()
    w2 = dt("w2", [E, F4, D], F32, kind="ExternalInput").ap()
    b2row = dt("b2row", [1, E * D], F32, kind="ExternalInput").ap()
    c_lt = dt("c_lt", [128, 128], F32, kind="ExternalInput").ap()
    c_io512 = dt("c_io512", [128, 512], F32, kind="ExternalInput").ap()
    c_io512b = dt("c_io512b", [128, 512], F32, kind="ExternalInput").ap()
    c_io8 = dt("c_io8", [128, 8], F32, kind="ExternalInput").ap()
    c_identf = dt("c_identf", [128, 128], F32, kind="ExternalInput").ap()
    c_identb = dt("c_identb", [128, 128], BF16, kind="ExternalInput").ap()
    c_mask = dt("c_mask", [128, 256], F32, kind="ExternalInput").ap()
    c_mask0 = dt("c_mask0", [128, 128], F32, kind="ExternalInput").ap()
    wrapm = dt("wrapm", [1, NCOL], F32, kind="ExternalInput").ap()
    c_vals = dt("c_vals", [128, 8], F32, kind="ExternalInput").ap()

    y = dt("y", [S, D], F32, kind="ExternalOutput").ap()
    d_xn = dt("d_xn", [S, D], BF16, kind="Internal").ap()
    d_attn = dt("d_attn", [S, D], F32, kind="Internal").ap()
    d_buf = dt("d_buf", [E, 8, 128, SLOT_E], BF16, kind="Internal").ap()
    d_w1b = dt("d_w1b", [E, D, F4], BF16, kind="Internal").ap()
    d_w2b = dt("d_w2b", [E, F4, D], BF16, kind="Internal").ap()
    d_ye = dt("d_ye", [E, NSC, 128, D], BF16, kind="Internal").ap()

    dbg = {}

    def dbgten(nm, shp, dd=F32):
        dbg[nm] = dt(nm, shp, dd, kind="ExternalOutput").ap()
        return dbg[nm]

    p = Prog()
    es = ExitStack()
    sbt = lambda n, s, d=F32: es.enter_context(nc.sbuf_tensor(n, s, d))
    sem = lambda n: es.enter_context(nc.semaphore(n))

    def sba(name, shape, dtyp, off):
        return nc.alloc_sbuf_tensor_at(name, shape, dtyp, offset=off).ap()

    with es:
        g9 = sbt("g9", [128, 8, 9], BF16)
        wg32 = sbt("wg32", [128, 8, 8], F32)
        onesb = sbt("onesb", [128, 1], BF16)
        onesf = sbt("onesf", [128, 1], F32)
        ones1f = sbt("ones1f", [1, 128], F32)
        ones18f = sbt("ones18f", [1, 8], F32)
        ltf = sbt("ltf", [128, 128], F32)
        io512 = sbt("io512", [128, 512], F32)
        io8 = sbt("io8", [128, 8], F32)
        identf = sbt("identf", [128, 128], F32)
        identb = sbt("identb", [128, 128], BF16)
        maskm = sbt("maskm", [128, 256], F32)
        mask0 = sbt("mask0", [128, 128], F32)
        wrapms = sbt("wrapms", [1, NCOL], F32)
        bq_s = sbt("bq_s", [128, 8], F32)
        bk_s = sbt("bk_s", [128, 8], F32)
        rowst = sbt("rowst", [1, D], F32)
        bg_s = sbt("bg_s", [8, 1], F32)
        negS = sbt("negS", [8, 1], F32)
        b1t = sbt("b1t", [128, 32, 8], F32)
        cvals = sbt("cvals", [128, 8], F32)
        idx1 = sbt("idx1", [128, NCOL], F32)
        idx2 = sbt("idx2", [128, NCOL], F32)
        dpl = sbt("dpl", [128, NCOL], F32)
        g1p = sbt("g1p", [128, NCOL], F32)
        g2p = sbt("g2p", [128, NCOL], F32)
        colf = sbt("colf", [128, 2, 32], F32)
        colov = sbt("colov", [128, 2, 32], F32)
        gsel = sbt("gsel", [128, 2, 32], F32)
        probe_h = sbt("probe_reg", [128, 32], F32)
        stack_top = nc.lookup_mloc(probe_h).addr + 128
        RB = (stack_top + 63) // 64 * 64
        RC = RB + 65536
        RD = RC + 57344
        assert RD + 65536 <= 229344, (stack_top, RB, RC, RD)
        wat = sba("wat", [128, 4, 8, D], BF16, RB)
        xnf = sba("xnf", [128, 8, S], BF16, RD)
        bvb = sba("bvb", [128, D], F32, RC + 47168)
        bob = sba("bob", [128, D], F32, RC + 51264)

        names = ["ld", "ldg", "tt", "vv", "ss", "pp", "st"]
        sml = {n: sem(n) for n in names}
        LD, LDG, TT, VV, SS, PP, ST = (sml[n] for n in names)

        def dma(eng_name, out_ap, in_ap, s=LD, waits=()):
            return p.add(eng_name, lambda e, o=out_ap, i=in_ap: e.dma_start(out=o, in_=i),
                         waits=waits, inc=(s, 16))

        def vop(fn, waits=(), s=VV):
            return p.add("vector", fn, waits=waits, inc=(s, 1))

        def sop(fn, waits=(), s=SS):
            return p.add("scalar", fn, waits=waits, inc=(s, 1))

        def gop(fn, waits=(), s=PP):
            return p.add("gpsimd", fn, waits=waits, inc=(s, 1))

        def top(fn, waits=(), s=TT):
            return p.add("tensor", fn, waits=waits, inc=(s, 1))

        def mm(out, lhsT, rhs, start, stop):
            return lambda t: t.matmul(out, lhsT, rhs, start=start, stop=stop)

        for dst, src in [(ltf, c_lt), (io512, c_io512), (io8, c_io8),
                         (identf, c_identf), (identb, c_identb), (maskm, c_mask), (mask0, c_mask0),
                         (bq_s, bqr), (bk_s, bkr)]:
            dma("sync", dst[:, :], src[:, :])
        dma("sync", wrapms[:, :], wrapm[:, :])
        dma("sync", bg_s[:, :], bgc[:, :])
        dma("sync", cvals[:, :], c_vals[:, :])
        dma("sync", b1t[:, :, :], b1r[:, :, :])

        dma("sync", wg32[:, :, :], wgi.rearrange("(c p) e -> p c e", p=128))
        ld_done = p.val(LD)

        ldg_done = p.val(LDG)

        vop(lambda v: v.memset(onesb[:], 1.0))
        vop(lambda v: v.memset(onesf[:], 1.0))
        vop(lambda v: v.memset(ones1f[:], 1.0))
        vop(lambda v: v.memset(ones18f[:], 1.0))
        vop(lambda v: v.memset(g9.ap().rearrange("p a b -> p (a b)"), 0.0))
        v_init = vop(lambda v: v.memset(g9[:, :, 8:9], 1.0), waits=[(LDG, ldg_done), (VV, p.val(VV))])
        dma("gpsimd", g9[:, :, 0:8], wgi.rearrange("(c p) e -> p c e", p=128), s=LDG,
            waits=[(VV, v_init)])
        ldg_done = p.val(LDG)

        es0 = ExitStack()
        pS = es0.enter_context(nc.psum_tensor("pS", [1, 512], F32))
        pS8 = es0.enter_context(nc.psum_tensor("pS8", [128, 512], F32))
        pbc = es0.enter_context(nc.psum_tensor("pbc", [128, 2, 512], F32))

        for ic in range(8):
            w = [(LD, ld_done), (VV, v_init)] if ic == 0 else ()
            t_s = top(mm(pS[0:1, 0:8], onesf[:, :], wg32[:, ic, :], ic == 0, ic == 7), waits=w)
        srow = sbt("srow", [1, 8], F32)
        v_sr = vop(lambda v: v.tensor_copy(srow[:, :], pS[0:1, 0:8]), waits=[(TT, t_s)])
        t_tr = top(mm(pS8[0:8, 0:8], srow[0:1, 0:8], identf[0:1, 0:8], True, True),
                   waits=[(VV, v_sr)])
        v_negs = vop(lambda v: v.tensor_scalar(negS[:, :], pS8[0:8, 0:1], -1.0, None, ALU.mult),
                     waits=[(TT, t_tr)])

        for bi, (row_src, dst) in enumerate([(bvrow, bvb), (borow, bob)]):
            lv = dma("sync", rowst[:, :], row_src[:, :],
                     waits=([(TT, p.val(TT)), (VV, p.val(VV))] if bi > 0 else ()))
            for half in range(2):
                t_b = top(mm(pbc[:, half, :], ones1f[0:1, :],
                             rowst[0:1, half * 512:(half + 1) * 512], True, True),
                          waits=[(LD, lv)])
                vop(lambda v, d=dst, h=half: v.tensor_copy(
                    d[:, h * 512:(h + 1) * 512], pbc[:, h, :]), waits=[(TT, t_b)])
        p0_tt = p.val(TT)
        p0_vv = p.val(VV)
        es0.close()

        xg = sba("xg", [128, 2, 8, 512], BF16, RB)
        sqt = sba("sqt", [128, 2, 2, 512], BF16, RB + 16384)
        lg9 = sba("lg9", [8, 2, 512], F32, RB + 24576)
        mus = sba("mus", [1, 2, 3, 512], F32, RB + 28672)
        ltt = sba("ltt", [8, 2, 512], F32, RB + 40960)
        l4 = sba("l4", [128, 2, 4, 8], F32, RB + 45056)
        scr = sba("scr", [128, 24], F32, RB + 45312)
        eq4 = sba("eq4", [128, 4, 8], F32, RB + 45440)
        tmp4 = sba("tmp4", [128, 4, 8], F32, RB + 45568)
        m14 = sba("m14", [128, 4], F32, RB + 45696)
        m24 = sba("m24", [128, 4], F32, RB + 45728)
        es1 = ExitStack()
        pL = es1.enter_context(nc.psum_tensor("pL", [8, 2, 512], F32))
        pQ = es1.enter_context(nc.psum_tensor("pQ", [1, 2, 512], F32))
        pB = es1.enter_context(nc.psum_tensor("pB", [8, 2, 512], F32))
        pT2 = es1.enter_context(nc.psum_tensor("pT2", [128, 2, 512], F32))

        NF = 64
        ldg_f = {}
        f_v3 = {}
        f_mml = {-2: p0_tt, -1: p0_tt}
        f_mmq = {-2: p0_tt, -1: p0_tt}
        f_vend = {-2: p0_vv, -1: p0_vv}
        f_trp = {-2: p0_tt, -1: p0_tt}
        for f in range(NF):
            b = f % 2
            for ic in range(8):
                w = [(TT, f_mmq[f - 2])] if ic == 0 else ()
                p.add("gpsimd", lambda e, bb=b, i=ic, ff=f: e.dma_start(
                    out=xg[:, bb, i, :], in_=xfm[i * 128:(i + 1) * 128, ff * 512:(ff + 1) * 512]),
                    waits=w, inc=(LDG, 16))
            ldg_f[f] = p.val(LDG)
            for ic in range(8):
                w = [(LDG, ldg_f[f]), (VV, f_vend[f - 2])] if ic == 0 else ()
                p.add("tensor", mm(pL[:, b, :], g9[:, ic, 0:8], xg[:, b, ic, :], ic == 0, ic == 7),
                      waits=w, inc=(TT, 1))
            for ic in range(8):
                w2_ = [(VV, f_v3[f - 1])] if (ic == 0 and f >= 1) else ()
                p.add("tensor", mm(pQ[:, 1, :], onesb[:, :], xg[:, b, ic, :], ic == 0, ic == 7),
                      waits=w2_, inc=(TT, 1))
            f_mml[f] = p.val(TT)
            qvals = []
            for ic in range(8):
                w = [(LDG, ldg_f[f])]
                if ic >= 2:
                    w.append((TT, qvals[ic - 2]))
                if ic == 0 and f >= 2:
                    w.append((TT, f_mmq[f - 2]))
                if ic == 0 and f >= 1:
                    w.append((VV, f_v3[f - 1]))
                s_q = sop(lambda s, bb=b, i=ic: s.activation(sqt[:, bb, i % 2, :], xg[:, bb, i, :], AF.Square),
                          waits=w)
                qvals.append(p.add("tensor", mm(pQ[:, 0, :], onesb[:, :], sqt[:, b, ic % 2, :], ic == 0, ic == 7),
                                   waits=[(SS, s_q)], inc=(TT, 1)))
            f_mmq[f] = qvals[-1]
            v1 = vop(lambda v, bb=b: v.tensor_copy(lg9[:, bb, :], pL[:, bb, :]), waits=[(TT, f_mml[f])])
            v2 = vop(lambda v, bb=b: v.tensor_scalar(mus[:, bb, 0, :], pQ[0:1, 1, :], 1.0 / 1024.0, None, ALU.mult))
            v3 = vop(lambda v, bb=b: v.tensor_scalar(mus[:, bb, 1, :], pQ[0:1, 0, :], 1.0 / 1024.0, None, ALU.mult),
                     waits=[(TT, f_mmq[f])])
            f_v3[f] = v3
            v4 = vop(lambda v, bb=b: v.tensor_tensor(mus[:, bb, 2, :], mus[:, bb, 0, :], mus[:, bb, 0, :], ALU.mult))
            v5 = vop(lambda v, bb=b: v.tensor_tensor(mus[:, bb, 1, :], mus[:, bb, 1, :], mus[:, bb, 2, :], ALU.subtract))
            s1 = sop(lambda s, bb=b: s.activation(mus[:, bb, 2, :], mus[:, bb, 1, :], AF.Ln, bias=cvals[0:1, 0:1], scale=1.0),
                     waits=[(VV, v5)])
            s2 = sop(lambda s, bb=b: s.activation(mus[:, bb, 2, :], mus[:, bb, 2, :], AF.Exp, bias=0.0, scale=cvals[0:1, 1:2]))
            t_mb = top(mm(pB[:, b, :], ones18f[0:1, 0:8], mus[0:1, b, 0, :], True, True),
                       waits=[(SS, s2)])
            v6 = vop(lambda v, bb=b: v.scalar_tensor_tensor(ltt[:, bb, :], pB[:, bb, :], negS[0:8, 0:1],
                                                            lg9[0:8, bb, :], ALU.mult, ALU.add),
                     waits=[(TT, t_mb)])
            t_rb = top(mm(pB[:, b, :], ones18f[0:1, 0:8], mus[0:1, b, 2, :], True, True),
                       waits=[(VV, v6)])
            v7 = vop(lambda v, bb=b: v.tensor_tensor(ltt[:, bb, :], ltt[:, bb, :], pB[:, bb, :], ALU.mult),
                     waits=[(TT, t_rb)])
            v8 = vop(lambda v, bb=b: v.tensor_scalar(ltt[:, bb, :], ltt[:, bb, :], bg_s[0:8, 0:1], None, ALU.add))
            for j in range(4):
                w = [(VV, v8)]
                if j == 0 and f >= 2:
                    w.append((VV, f_vend[f - 2]))
                p.add("tensor", lambda t, bb=b, jj=j: t.transpose(
                    pT2[:, bb, jj * 8:(jj + 1) * 8], ltt[0:8, bb, jj * 128:(jj + 1) * 128], identf[0:8, 0:8]),
                    waits=(w if j == 0 else ()), inc=((TT, 1) if j == 3 else None))
            f_trp[f] = p.val(TT)
            v9 = vop(lambda v, bb=b: v.tensor_copy(l4[:, bb, :, :],
                                                   pT2[:, bb, 0:32].rearrange("p (a b) -> p a b", a=4)),
                     waits=[(TT, f_trp[f])])
            for j in range(4):
                vop(lambda v, bb=b, jj=j: v.reduce_max(m14[:, jj:jj + 1], l4[:, bb, jj, :], axis=AX.X))
            for j in range(4):
                vop(lambda v, bb=b, jj=j: v.tensor_scalar(eq4[:, jj, :], l4[:, bb, jj, :],
                                                          m14[:, jj:jj + 1], None, ALU.is_equal))
            for j in range(4):
                vop(lambda v, jj=j: v.tensor_tensor(tmp4[:, jj, :], eq4[:, jj, :], io8[:, :], ALU.mult))
            for j in range(4):
                vop(lambda v, jj=j, c=4 * f + j: v.reduce_sum(idx1[:, c:c + 1], tmp4[:, jj, :], axis=AX.X))
            for j in range(4):
                vop(lambda v, bb=b, jj=j: v.scalar_tensor_tensor(tmp4[:, jj, :], eq4[:, jj, :], -BIG,
                                                                 l4[:, bb, jj, :], ALU.mult, ALU.add))
            for j in range(4):
                vop(lambda v, jj=j: v.reduce_max(m24[:, jj:jj + 1], tmp4[:, jj, :], axis=AX.X))
            for j in range(4):
                vop(lambda v, jj=j: v.tensor_scalar(eq4[:, jj, :], tmp4[:, jj, :],
                                                    m24[:, jj:jj + 1], None, ALU.is_equal))
            for j in range(4):
                vop(lambda v, jj=j: v.tensor_tensor(tmp4[:, jj, :], eq4[:, jj, :], io8[:, :], ALU.mult))
            for j in range(4):
                vop(lambda v, jj=j, c=4 * f + j: v.reduce_sum(idx2[:, c:c + 1], tmp4[:, jj, :], axis=AX.X))
            for j in range(4):
                vop(lambda v, jj=j, c=4 * f + j: v.tensor_tensor(dpl[:, c:c + 1], m24[:, jj:jj + 1],
                                                                 m14[:, jj:jj + 1], ALU.subtract))
            f_vend[f] = p.val(VV)
        s_g = sop(lambda s: s.activation(g2p[:, :], dpl[:, :], AF.Exp, bias=0.0, scale=1.0),
                  waits=[(VV, p.val(VV))])
        vop(lambda v: v.tensor_scalar(g2p[:, :], g2p[:, :], 1.0, None, ALU.add), waits=[(SS, s_g)])
        vop(lambda v: v.tensor_copy(scr[:, 0:1], onesf[:, :]))
        vop(lambda v: v.reciprocal(g1p[:, :], g2p[:, :]))
        vop(lambda v: v.tensor_copy(scr[:, 0:1], onesf[:, :]))
        vop(lambda v: v.tensor_scalar(g2p[:, :], g1p[:, :], -1.0, 1.0, ALU.mult, ALU.add))
        gates_done = p.val(VV)
        es1.close()
        p1_tt = p.val(TT)

        ind = sba("ind", [128, 2, 8, NCOL], F32, RD)
        exc = sba("exc", [128, 2, 8, NCOL], F32, RD + 16384)
        tot = sba("tot", [1, 2, 8, NCOL], F32, RD + 32768)
        cof = sba("cof", [1, 2, 8, NCOL], F32, RD + 49152)
        rke = sba("rke", [128, 2, 8, 32], F32, RC)
        ove = sba("ove", [128, 2, 8, 32], F32, RC + 8192)
        kco = sba("kco", [128, 2, 8, 32], F32, RC + 16384)
        ids = sba("ids", [128, 2, 8, 32], F32, RC + 24576)
        wrow = sba("wrow", [1, 2 * 8 * 4], F32, RC + 32768)
        ovsc = sba("ovsc", [1, 2, 8, 32], F32, RC + 33024)
        ovco = sba("ovco", [1, 2, 8, 32], F32, RC + 36096)
        cn0 = sba("cn0", [1, 8, 32], F32, RC + 39168)

        es2 = ExitStack()
        pE = es2.enter_context(nc.psum_tensor("pE", [128, 2, 512], F32))
        pCt = es2.enter_context(nc.psum_tensor("pCt", [1, 2, 512], F32))
        pBC = es2.enter_context(nc.psum_tensor("pBC", [128, 2, 512], F32))

        pv0 = p.val(VV)
        for k in range(2):
            for e in range(E):
                src_pl = idx1 if k == 0 else idx2
                vop(lambda v, kk=k, ee=e, sp=src_pl: v.tensor_scalar(
                    ind[:, kk, ee, :], sp[:, :], float(ee), None, ALU.is_equal),
                    waits=([(VV, gates_done)] if (k == 0 and e == 0) else ()))
        ind_done = p.val(VV)
        mme = {}
        for k in range(2):
            for e in range(E):
                i = k * 8 + e
                w = [(VV, ind_done)] if i == 0 else []
                if i >= 2:
                    w.append((VV, mme[i - 2]))
                p.add("tensor", mm(pE[:, i % 2, 0:256], ltf[:, :], ind[:, k, e, :], True, True),
                      waits=w, inc=(TT, 1))
                p.add("tensor", mm(pCt[:, i % 2, 0:256], onesf[:, :], ind[:, k, e, :], True, True),
                      inc=(TT, 1))
                tvv = p.val(TT)
                vop(lambda v, kk=k, ee=e, bb=i % 2: v.tensor_copy(exc[:, kk, ee, :], pE[:, bb, 0:256]),
                    waits=[(TT, tvv)])
                mme[i] = vop(lambda v, kk=k, ee=e, bb=i % 2: v.tensor_copy(tot[:, kk, ee, :], pCt[:, bb, 0:256]))
        v_w1 = vop(lambda v: v.tensor_tensor(cof.rearrange("a b c d -> a (b c) d"),
                                             tot.rearrange("a b c d -> a (b c) d"),
                                             wrapms[0:1, :].rearrange("a (b d) -> a b d", b=1).broadcast(1, 16) if False else tot.rearrange("a b c d -> a (b c) d"), ALU.mult))
        p.ops["vector"].pop()
        p.cnt[VV] -= 1
        for k in range(2):
            for e in range(E):
                vop(lambda v, kk=k, ee=e: v.tensor_tensor(cof[:, kk, ee, :], tot[:, kk, ee, :],
                                                          wrapms[:, :], ALU.mult))
        v_w3 = vop(lambda v: v.reduce_sum(wrow[0:1, 16:24].rearrange("a (b c) -> a b c", b=8),
                                          tot[:, 0, :, :], axis=AX.X))
        v_w2 = vop(lambda v: v.reduce_sum(wrow[0:1, 0:16].rearrange("a (b c) -> a b c", b=16),
                                          cof.rearrange("a b c d -> a (b c) d"), axis=AX.X))
        v_cs = []
        for k in range(2):
            for e in range(E):
                vop(lambda v, kk=k, ee=e: v.memset(cof[:, kk, ee, 0:1], 0.0))
        v_w4 = vop(lambda v: v.tensor_tensor(wrow[0:1, 8:16], wrow[0:1, 8:16], wrow[0:1, 16:24], ALU.add))
        for k in range(2):
            for e in range(E):
                vop(lambda v, kk=k, ee=e: v.tensor_tensor_scan(
                    cof[:, kk, ee, 1:NCOL], tot[:, kk, ee, 0:NCOL - 1], tot[:, kk, ee, 0:NCOL - 1],
                    0.0, ALU.add, ALU.bypass))
        for k in range(2):
            for e in range(E):
                i = k * 8 + e
                vop(lambda v, kk=k, ee=e, ii=i: v.tensor_scalar(
                    tot[:, kk, ee, :], cof[:, kk, ee, :], wrow[0:1, ii:ii + 1], None, ALU.add))
        for k in range(2):
            for e in range(E):
                vop(lambda v, kk=k, ee=e: v.tensor_scalar(
                    tot[:, kk, ee, :], tot[:, kk, ee, :], -1.0, float(CAP), ALU.mult, ALU.add))
        thr_done = p.val(VV)
        if debug >= 1:
            o_w = dbgten("g_wrow", [1, 32])
            p.add("sync", lambda e, oo=o_w: e.dma_start(out=oo[:, :], in_=wrow[0:1, 0:32]),
                  waits=[(VV, thr_done)], inc=(ST, 16))
            o_t = dbgten("g_thr", [1, 16 * NCOL])
            p.add("sync", lambda e, oo=o_t: e.dma_start(
                out=oo[:, :], in_=tot.rearrange("a b c d -> a (b c d)")),
                waits=[(VV, thr_done)], inc=(ST, 16))
            o_e = dbgten("g_exc", [128, 16 * NCOL])
            p.add("sync", lambda e, oo=o_e: e.dma_start(
                out=oo[:, :], in_=exc.rearrange("p b c d -> p (b c d)")),
                waits=[(VV, thr_done)], inc=(ST, 16))
        for k in range(2):
            for e in range(E):
                i = k * 8 + e
                w = ([(VV, thr_done)] + ([(ST, p.val(ST))] if debug >= 1 else [])) if i == 0 else []
                if i >= 2:
                    w.append((VV, kdone[i - 2]))
                t_bb = p.add("tensor", mm(pBC[:, i % 2, 0:256], ones1f[0:1, :], tot[0:1, k, e, :], True, True),
                             waits=w, inc=(TT, 1))
                vop(lambda v, kk=k, ee=e, bb=i % 2: v.tensor_tensor(
                    exc[:, kk, ee, :], exc[:, kk, ee, :], pBC[:, bb, 0:256], ALU.is_lt),
                    waits=[(TT, t_bb)])
                if i == 0:
                    kdone = {}
                kdone[i] = vop(lambda v, kk=k, ee=e: v.tensor_tensor(
                    ind[:, kk, ee, :], ind[:, kk, ee, :], exc[:, kk, ee, :], ALU.mult))
        keep_done = p.val(VV)
        rdone = {}
        for k in range(2):
            for e in range(E):
                i = k * 8 + e
                w = [(VV, keep_done)] if i == 0 else []
                if i >= 2:
                    w.append((VV, rdone[i - 2]))
                p.add("tensor", mm(pE[:, i % 2, 0:32], ltf[:, :], ind[:, k, e, 0:32], True, True),
                      waits=w, inc=(TT, 1))
                p.add("tensor", mm(pCt[:, i % 2, 0:32], onesf[:, :], ind[:, k, e, 0:32], True, True),
                      inc=(TT, 1))
                tvv = p.val(TT)
                vop(lambda v, kk=k, ee=e, bb=i % 2: v.tensor_copy(rke[:, kk, ee, :], pE[:, bb, 0:32]),
                    waits=[(TT, tvv)])
                vop(lambda v, kk=k, ee=e: v.tensor_copy(ids[:, kk, ee, :], ind[:, kk, ee, 0:32]))
                if k == 0:
                    rdone[i] = vop(lambda v, ee=e, bb=i % 2: v.tensor_copy(cn0[:, ee, :], pCt[:, bb, 0:32]))
                else:
                    rdone[i] = vop(lambda v: v.tensor_copy(wrow[0:1, 24:25], onesf[0:1, 0:1]))
        r1done = p.val(VV)
        c1done = {}
        for e in range(E):
            w = [(VV, r1done)] if e == 0 else []
            if e >= 2:
                w.append((VV, c1done[e - 2]))
            t_bb = p.add("tensor", mm(pBC[:, e % 2, 0:32], ones1f[0:1, :], cn0[0:1, e, :], True, True),
                         waits=w, inc=(TT, 1))
            c1done[e] = vop(lambda v, ee=e, bb=e % 2: v.tensor_tensor(
                rke[:, 1, ee, :], rke[:, 1, ee, :], pBC[:, bb, 0:32], ALU.add),
                waits=[(TT, p.val(TT))])
        rank_done = p.val(VV)
        for k in range(2):
            for e in range(E):
                vop(lambda v, kk=k, ee=e: v.tensor_scalar(
                    exc[:, kk, ee, 0:32], rke[:, kk, ee, :], float(W), None, ALU.is_ge),
                    waits=([(VV, rank_done)] if (k == 0 and e == 0) else ()))
        for k in range(2):
            for e in range(E):
                vop(lambda v, kk=k, ee=e: v.tensor_tensor(
                    exc[:, kk, ee, 0:32], exc[:, kk, ee, 0:32], ind[:, kk, ee, 0:32], ALU.mult))
        ov_done = p.val(VV)
        odone = {}
        for k in range(2):
            for e in range(E):
                i = k * 8 + e
                w = [(VV, ov_done)] if i == 0 else []
                if i >= 2:
                    w.append((VV, odone[i - 2]))
                p.add("tensor", mm(pE[:, i % 2, 0:32], ltf[:, :], exc[:, k, e, 0:32], True, True),
                      waits=w, inc=(TT, 1))
                p.add("tensor", mm(pCt[:, i % 2, 0:32], onesf[:, :], exc[:, k, e, 0:32], True, True),
                      inc=(TT, 1))
                tvv = p.val(TT)
                vop(lambda v, kk=k, ee=e, bb=i % 2: v.tensor_copy(ove[:, kk, ee, :], pE[:, bb, 0:32]),
                    waits=[(TT, tvv)])
                odone[i] = vop(lambda v, kk=k, ee=e, bb=i % 2: v.tensor_copy(ovsc[:, kk, ee, :], pCt[:, bb, 0:32]))
        ovt_done = p.val(VV)
        for e in range(E):
            vop(lambda v, ee=e: v.memset(ovco[:, 0, ee, 0:1], 0.0),
                waits=([(VV, ovt_done)] if e == 0 else ()))
        for e in range(E):
            vop(lambda v, ee=e: v.tensor_tensor_scan(
                ovco[:, 0, ee, 1:32], ovsc[:, 0, ee, 0:31], ovsc[:, 0, ee, 0:31], 0.0, ALU.add, ALU.bypass))
        for e in range(E):
            vop(lambda v, ee=e: v.tensor_tensor(wrow[0:1, 32 + ee:33 + ee], ovco[0:1, 0, ee, 31:32],
                                                ovsc[0:1, 0, ee, 31:32], ALU.add))
        for e in range(E):
            vop(lambda v, ee=e: v.tensor_copy(ovco[:, 1, ee, 0:1], wrow[0:1, 32 + ee:33 + ee]))
        for e in range(E):
            vop(lambda v, ee=e: v.tensor_tensor_scan(
                ovco[:, 1, ee, 1:32], ovsc[:, 1, ee, 0:31], ovsc[:, 1, ee, 0:31], 0.0, ALU.add, ALU.bypass))
        for e in range(E):
            vop(lambda v, ee=e: v.tensor_scalar(
                ovco[:, 1, ee, 1:32], ovco[:, 1, ee, 1:32], wrow[0:1, 32 + ee:33 + ee], None, ALU.add))
        ovco_done = p.val(VV)
        obdone = {}
        for k in range(2):
            for e in range(E):
                i = k * 8 + e
                w = [(VV, ovco_done)] if i == 0 else []
                if i >= 2:
                    w.append((VV, obdone[i - 2]))
                t_bb = p.add("tensor", mm(pBC[:, i % 2, 0:32], ones1f[0:1, :], ovco[0:1, k, e, :], True, True),
                             waits=w, inc=(TT, 1))
                obdone[i] = vop(lambda v, kk=k, ee=e, bb=i % 2: v.tensor_tensor(
                    ove[:, kk, ee, :], ove[:, kk, ee, :], pBC[:, bb, 0:32], ALU.add),
                    waits=[(TT, p.val(TT))])
        ovr_done = p.val(VV)
        prods = sba("prods", [128, 2, 32, 8], F32, RD + 49152)
        for k in range(2):
            for e in range(E):
                vop(lambda v, kk=k, ee=e: v.tensor_tensor(
                    prods[:, kk, :, ee], ids[:, kk, ee, :], rke[:, kk, ee, :], ALU.mult),
                    waits=([(VV, ovr_done)] if (k == 0 and e == 0) else ()))
        for k in range(2):
            vop(lambda v, kk=k: v.reduce_sum(colf[:, kk, :], prods[:, kk, :, :], axis=AX.X))
        for k in range(2):
            for e in range(E):
                vop(lambda v, kk=k, ee=e: v.tensor_tensor(
                    prods[:, kk, :, ee], ids[:, kk, ee, :], ove[:, kk, ee, :], ALU.mult))
        for k in range(2):
            vop(lambda v, kk=k: v.reduce_sum(colov[:, kk, :], prods[:, kk, :, :], axis=AX.X))
        for k in range(2):
            for e in range(E):
                vop(lambda v, kk=k, ee=e: v.tensor_copy(prods[:, kk, :, ee], ids[:, kk, ee, :]))
        for k in range(2):
            vop(lambda v, kk=k: v.reduce_sum(kco[:, kk, 0, :], prods[:, kk, :, :], axis=AX.X))
        sel_done = p.val(VV)
        for k in range(2):
            vop(lambda v, kk=k: v.tensor_scalar(kco[:, kk, 2, :], colf[:, kk, :], float(W), None, ALU.is_lt),
                waits=([(VV, sel_done)] if k == 0 else ()))
        for k in range(2):
            vop(lambda v, kk=k: v.tensor_tensor(kco[:, kk, 2, :], kco[:, kk, 2, :], kco[:, kk, 0, :], ALU.mult))
        for k in range(2):
            vop(lambda v, kk=k: v.tensor_scalar(kco[:, kk, 3, :], colov[:, kk, :], float(OV), None, ALU.is_lt))
        for k in range(2):
            vop(lambda v, kk=k: v.tensor_tensor(kco[:, kk, 3, :], kco[:, kk, 3, :], kco[:, kk, 0, :], ALU.mult))
        for k in range(2):
            vop(lambda v, kk=k: v.tensor_scalar(kco[:, kk, 4, :], colf[:, kk, :], float(W), None, ALU.is_ge))
        for k in range(2):
            vop(lambda v, kk=k: v.tensor_tensor(kco[:, kk, 3, :], kco[:, kk, 3, :], kco[:, kk, 4, :], ALU.mult))
        for k in range(2):
            ip = idx1 if k == 0 else idx2
            vop(lambda v, kk=k, ipl=ip: v.scalar_tensor_tensor(
                kco[:, kk, 6, :], ipl[:, 0:32], float(W), colf[:, kk, :], ALU.mult, ALU.add))
        for k in range(2):
            vop(lambda v, kk=k: v.tensor_scalar(kco[:, kk, 5, :], kco[:, kk, 2, :], -BIG, BIG, ALU.mult, ALU.add))
        for k in range(2):
            ip = idx1 if k == 0 else idx2
            vop(lambda v, kk=k, ipl=ip: v.scalar_tensor_tensor(
                kco[:, kk, 1, :], ipl[:, 0:32], float(OV), colov[:, kk, :], ALU.mult, ALU.add))
        for k in range(2):
            vop(lambda v, kk=k: v.tensor_tensor(colf[:, kk, :], kco[:, kk, 6, :], kco[:, kk, 5, :], ALU.add))
        for k in range(2):
            vop(lambda v, kk=k: v.tensor_scalar(kco[:, kk, 7, :], kco[:, kk, 3, :], -BIG, BIG, ALU.mult, ALU.add))
        for k in range(2):
            vop(lambda v, kk=k: v.tensor_tensor(colov[:, kk, :], kco[:, kk, 1, :], kco[:, kk, 7, :], ALU.add))
        for k in range(2):
            gp = g1p if k == 0 else g2p
            vop(lambda v, kk=k, g=gp: v.tensor_copy(gsel[:, kk, :], g[:, 0:32]))
        p2_done = p.val(VV)
        es2.close()
        p2_tt = p.val(TT)
        if debug >= 1:
            for nm, srcp in [("g_colf", colf), ("g_colov", colov), ("g_gsel", gsel)]:
                o = dbgten(nm, [128, 64])
                p.add("sync", lambda e, oo=o, ss_=srcp: e.dma_start(
                    out=oo[:, :], in_=ss_.rearrange("p a b -> p (a b)")),
                    waits=[(VV, p2_done)], inc=(ST, 16))
        if debug >= 2:
            o = dbgten("g_kept", [128, 16 * NCOL])
            p.add("sync", lambda e, oo=o: e.dma_start(
                out=oo[:, :], in_=ind.rearrange("p a b c -> p (a b c)")),
                waits=[(VV, p2_done)], inc=(ST, 16))
            o2 = dbgten("g_rke", [128, 2 * 8 * 32])
            p.add("sync", lambda e, oo=o2: e.dma_start(
                out=oo[:, :], in_=rke.rearrange("p a b c -> p (a b c)")),
                waits=[(VV, p2_done)], inc=(ST, 16))
            o4 = dbgten("g_kco", [128, 2 * 8 * 32])
            p.add("sync", lambda e, oo=o4: e.dma_start(
                out=oo[:, :], in_=kco.rearrange("p a b c -> p (a b c)")),
                waits=[(VV, p2_done)], inc=(ST, 16))
            o3 = dbgten("g_ove", [128, 2 * 8 * 32])
            p.add("sync", lambda e, oo=o3: e.dma_start(
                out=oo[:, :], in_=ove.rearrange("p a b c -> p (a b c)")),
                waits=[(VV, p2_done)], inc=(ST, 16))
        watw = [(VV, gates_done), (TT, p1_tt), (SS, p.val(SS))]
        for wi, wsrc in enumerate([wq, wk, wv, wo]):
            for ic in range(8):
                dma("gpsimd", wat[:, wi, ic, :], wsrc[ic * 128:(ic + 1) * 128, :], s=LDG,
                    waits=(watw if (wi == 0 and ic == 0) else ()))
        wat_done = p.val(LDG)

        if debug >= 2:
            for nm, src, shp in [("g_lg9", lg9, [8, 2 * 512]), ("g_mus", mus, [1, 2 * 3 * 512]),
                                 ("g_ltt", ltt, [8, 2 * 512]), ("g_l4", l4, [128, 2 * 4 * 8])]:
                o = dbgten(nm, shp)
                p.add("sync", lambda e, oo=o, ss_=src: e.dma_start(
                    out=oo[:, :], in_=ss_.rearrange("p a b -> p (a b)") if len(ss_.shape) == 3 else ss_.rearrange("p a b c -> p (a b c)")),
                    waits=[(VV, gates_done)], inc=(ST, 16))
        if debug >= 1:
            for nm, src in [("g_idx1", idx1), ("g_idx2", idx2), ("g_g1", g1p), ("g_g2", g2p),
                            ("g_dpl", dpl)]:
                o = dbgten(nm, [128, NCOL])
                p.add("sync", lambda e, oo=o, ss_=src: e.dma_start(out=oo[:, :], in_=ss_[:, :]),
                      waits=[(VV, gates_done)], inc=(ST, 16))

        p.add("sync", lambda e: e.nop(),
              waits=[(ST, p.val(ST)), (VV, p.val(VV)), (SS, p.val(SS)), (TT, p.val(TT)),
                     (LDG, p.val(LDG)), (LD, p.val(LD)), (PP, p.val(PP))])


        xld = sba("xld", [128, 2, D], F32, RC)
        sqx = sba("sqx", [128, 2, D], F32, RC + 8192)
        st3 = sba("st3", [128, 2, 8], F32, RC + 16384)
        xnt = sba("xnt", [128, 2, D], BF16, RC + 16896)
        es3 = ExitStack()
        pTR = es3.enter_context(nc.psum_tensor("pTR", [128, 2, 1024], BF16))
        t3_ld = {}
        t3_sq = {}
        t3_v1 = {}
        t3_xn = {}
        t3_tp = {}
        t3_ev = {-2: 0, -1: 0}
        t3_st = {}
        for Ti in range(NTILE):
            b = Ti % 2
            w = [(VV, p2_done)] if Ti == 0 else []
            if Ti >= 2:
                w.append((VV, t3_xn[Ti - 2]))
            t3_ld[Ti] = dma("sync", xld[:, b, :], x_sh[Ti * 128:(Ti + 1) * 128, :], waits=w)
            v_s = vop(lambda v, bb=b: v.reduce_sum(st3[:, bb, 0:1], xld[:, bb, :], axis=AX.X),
                      waits=[(LD, t3_ld[Ti])])
            s_sq = sop(lambda s, bb=b: s.activation(sqx[:, bb, :], xld[:, bb, :], AF.Square),
                       waits=[(LD, t3_ld[Ti])] + ([(VV, t3_v1[Ti - 2])] if Ti >= 2 else []))
            v_s2 = vop(lambda v, bb=b: v.reduce_sum(st3[:, bb, 1:2], sqx[:, bb, :], axis=AX.X),
                       waits=[(SS, s_sq)])
            v_mu = vop(lambda v, bb=b: v.tensor_scalar(st3[:, bb, 2:3], st3[:, bb, 0:1], 1.0 / 1024.0, None, ALU.mult))
            v_e2 = vop(lambda v, bb=b: v.tensor_scalar(st3[:, bb, 3:4], st3[:, bb, 1:2], 1.0 / 1024.0, None, ALU.mult))
            v_m2 = vop(lambda v, bb=b: v.tensor_tensor(st3[:, bb, 4:5], st3[:, bb, 2:3], st3[:, bb, 2:3], ALU.mult))
            vop(lambda v, bb=b: v.tensor_copy(sqx[:, bb, 0:1], onesf[:, :]))
            v_va = vop(lambda v, bb=b: v.tensor_tensor(st3[:, bb, 5:6], st3[:, bb, 3:4], st3[:, bb, 4:5], ALU.subtract))
            t3_v1[Ti] = v_va
            s_ln = sop(lambda s, bb=b: s.activation(st3[:, bb, 6:7], st3[:, bb, 5:6], AF.Ln,
                                                    bias=cvals[:, 0:1], scale=1.0), waits=[(VV, v_va)])
            s_rs = sop(lambda s, bb=b: s.activation(st3[:, bb, 7:8], st3[:, bb, 6:7], AF.Exp,
                                                    bias=0.0, scale=cvals[:, 1:2]))
            t3_xn[Ti] = vop(lambda v, bb=b: v.tensor_scalar(xnt[:, bb, :], xld[:, bb, :],
                                                            st3[:, bb, 2:3], st3[:, bb, 7:8],
                                                            ALU.subtract, ALU.mult),
                            waits=[(SS, s_rs)])
            dma("sync", d_xn[Ti * 128:(Ti + 1) * 128, :], xnt[:, b, :], waits=[(VV, t3_xn[Ti])], s=ST)
            for ic in range(8):
                ww = [(VV, t3_xn[Ti])] if ic == 0 else []
                if ic == 0 and Ti >= 2:
                    ww.append((VV, t3_ev[Ti - 2]))
                p.add("tensor", lambda t, bb=b, i=ic: t.transpose(
                    pTR[:, bb, i * 128:(i + 1) * 128], xnt[:, bb, i * 128:(i + 1) * 128], identb[:, :]),
                    waits=ww, inc=((TT, 1) if ic == 7 else None))
            t3_tp[Ti] = p.val(TT)
            for ic in range(8):
                vop(lambda v, bb=b, i=ic, tt_=Ti: v.tensor_copy(
                    xnf[:, i, tt_ * 128:(tt_ + 1) * 128], pTR[:, bb, i * 128:(i + 1) * 128]),
                    waits=([(TT, t3_tp[Ti])] if ic == 0 else ()))
            t3_ev[Ti] = p.val(VV)
        p3_done = p.val(VV)
        p3_tt = p.val(TT)
        es3.close()

        qf = sba("qf", [128, 8, 512], BF16, RC)
        kf = sba("kf", [128, 8, 640], BF16, RC + 8192)
        vt4 = sba("vt4", [128, 5, D], BF16, RC + 18432)
        apre = sba("apre", [128, 8, 512], BF16, RC + 28672)
        amsk = sba("amsk", [128, 2, 256], F32, RC + 36864)
        aexp = sba("aexp", [128, 2, 256], F32, RC + 38912)
        aprb = sba("aprb", [128, 2, 2, 256], BF16, RC + 40960)
        apT = sba("apT", [128, 2, 256], BF16, RC + 55360)
        asml = sba("asml", [128, 2, 8], F32, RC + 43008)
        aout1 = sba("aout1", [128, 2, 512], F32, RC + 43072)
        es4 = ExitStack()
        pQK = es4.enter_context(nc.psum_tensor("pQK", [128, 2, 512], F32))
        pSC = es4.enter_context(nc.psum_tensor("pSC", [128, 2, 512], F32))
        pPT = es4.enter_context(nc.psum_tensor("pPT", [128, 2, 1024], BF16))
        pAV = es4.enter_context(nc.psum_tensor("pAV", [128, 2, 512], F32))
        NMC = 8
        P4PART = 3
        MAXPH = 9
        P7PART = 4
        pSC_free = {0: 0, 1: 0}
        pPT_free = {0: 0, 1: 0}
        ost_free = {}
        mc_ttdone = {-1: p.val(TT)}
        for mc in range(NMC):
            t0 = mc * 512
            kbase = t0 - 128
            ev_q = {}
            for oc in range(8):
                bb = oc % 2
                for icc in range(8):
                    w = []
                    if oc == 0 and icc == 0:
                        w = [(VV, p3_done), (LDG, wat_done), (TT, mc_ttdone[mc - 1])]
                        if mc > 0:
                            w.append((SS, mc_ssdone))
                            w.append((VV, mc_vvdone))
                    if icc == 0 and oc >= 2:
                        w.append((SS, ev_q[oc - 2]))
                    p.add("tensor", mm(pQK[:, bb, :], wat[:, 0, icc, oc * 128:(oc + 1) * 128],
                                       xnf[:, icc, t0:t0 + 512], icc == 0, icc == 7),
                          waits=w, inc=((TT, 1) if icc == 7 else None))
                ev_q[oc] = sop(lambda s, b2=bb, o2=oc: s.activation(
                    qf[:, o2, :], pQK[:, b2, :], AF.Identity, bias=bq_s[:, o2:o2 + 1], scale=1.0),
                    waits=[(TT, p.val(TT))])
            ev_k = {}
            pieces = [(0, 320), (320, 320)] if mc > 0 else [(128, 256), (384, 256)]
            for oc in range(8):
                bb = oc % 2
                for pi, (cst, cw) in enumerate(pieces):
                    for icc in range(8):
                        w = []
                        if icc == 0 and pi == 0 and oc == 0:
                            w.append((SS, ev_q[7]))
                        if icc == 0 and pi == 0 and oc >= 2:
                            w.append((SS, ev_k[oc - 2]))
                        if icc == 0 and pi == 1:
                            w.append((SS, ev_k[oc]))
                        p.add("tensor", mm(pQK[:, bb, 0:cw],
                                           wat[:, 1, icc, oc * 128:(oc + 1) * 128],
                                           xnf[:, icc, kbase + cst:kbase + cst + cw], icc == 0, icc == 7),
                              waits=w, inc=((TT, 1) if icc == 7 else None))
                    ev_k[oc] = sop(lambda s, b2=bb, o2=oc, c2=cst, w2=cw: s.activation(
                        kf[:, o2, c2:c2 + w2], pQK[:, b2, 0:w2], AF.Identity, bias=bk_s[:, o2:o2 + 1], scale=1.0),
                        waits=[(TT, p.val(TT))])
            ev_v = {}
            vts = [0, 1, 2, 3, 4] if mc > 0 else [1, 2, 3, 4]
            for vt_i in vts:
                bb = vt_i % 2
                tok = kbase + vt_i * 128
                for half in range(2):
                    for icc in range(8):
                        w = []
                        if icc == 0 and half == 0 and vt_i == vts[0]:
                            w.append((SS, ev_k[7]))
                        if icc == 0 and half == 0 and ev_v.get(vt_i - 2):
                            w.append((VV, ev_v[vt_i - 2]))
                        if icc == 0 and half == 1:
                            w.append((VV, p.val(VV)))
                        p.add("tensor", mm(pQK[:, bb, :], xnf[:, icc, tok:tok + 128],
                                           wat[:, 2, icc, half * 512:(half + 1) * 512], icc == 0, icc == 7),
                              waits=w, inc=((TT, 1) if icc == 7 else None))
                    vop(lambda v, b2=bb, v2=vt_i, h2=half: v.tensor_tensor(
                        vt4[:, v2, h2 * 512:(h2 + 1) * 512], pQK[:, b2, :],
                        bvb[:, h2 * 512:(h2 + 1) * 512], ALU.add),
                        waits=[(TT, p.val(TT))])
                ev_v[vt_i] = p.val(VV)
            proj_done_v = p.val(VV)
            proj_done_s = p.val(SS)
            av_ev = {}
            prev_exp = {}
            prev_prb = {}
            tp_par = {}
            pend = None

            def emit_tpav(pe):
                w4p, gwp, kwidp, hpp, prbp, pbp = pe
                nkc = kwidp // 128
                tp_t = {}
                for hh in range(2):
                    for kc in range(nkc):
                        p.add("tensor", lambda t, b2=hh, k2=kc, pb2=pbp: t.transpose(
                            pPT[:, b2, k2 * 128:(k2 + 1) * 128],
                            aprb[:, pb2, b2, k2 * 128:(k2 + 1) * 128], identb[:, :]),
                            waits=([(SS, prbp[hh]), (VV, pPT_free[hh])] if kc == 0 else ()),
                            inc=((TT, 1) if kc == nkc - 1 else None))
                    tp_t[hh] = p.val(TT)
                tp_par[pbp] = tp_t[1]
                for hh in range(2):
                    vop(lambda v, b2=hh, kw=kwidp: v.tensor_copy(
                        apT[:, b2, 0:kw], pPT[:, b2, 0:kw]), waits=[(TT, tp_t[hh])])
                evT = p.val(VV)
                pPT_free[0] = pPT_free[1] = evT
                for hh in range(2):
                    h = hpp * 2 + hh
                    for kc in range(nkc):
                        vt_idx = (w4p + kc) if gwp > 0 else 1
                        w = [(VV, evT)] if (hh == 0 and kc == 0) else []
                        if hh == 0 and kc == 0 and av_ev.get((w4p, hpp - 1)):
                            w.append((VV, av_ev[(w4p, hpp - 1)]))
                        if hh == 0 and kc == 0 and w4p > 0 and av_ev.get((w4p - 1, hpp)):
                            w.append((VV, av_ev[(w4p - 1, hpp)]))
                        p.add("tensor", mm(
                            pAV[64 * (h % 2):64 * (h % 2) + 64, hpp % 2, 0:128],
                            vt4[:, vt_idx, 64 * h:64 * h + 64],
                            apT[:, hh, kc * 128:(kc + 1) * 128],
                            kc == 0, kc == nkc - 1), waits=w,
                            inc=((TT, 1) if (hh == 1 and kc == nkc - 1) else None))
                av_tt = p.val(TT)
                av_ev[(w4p, hpp)] = vop(lambda v, b2=hpp % 2, o2=hpp, w2=w4p: v.tensor_copy(
                    apre[:, o2, w2 * 128:(w2 + 1) * 128], pAV[:, b2, 0:128]),
                    waits=[(TT, av_tt)])

            for w4 in range(4 if P4PART >= 2 else 0):
                gw = mc * 4 + w4
                mvar = mask0 if gw == 0 else maskm
                kcol = w4 * 128 if gw > 0 else 128
                kwid = 256 if gw > 0 else 128
                for hp in range(8):
                    pb4 = (w4 * 8 + hp) % 2
                    sc_t = {}
                    for hh in range(2):
                        h = hp * 2 + hh
                        w = [(VV, pSC_free[hh])]
                        if w4 == 0 and hp == 0 and hh == 0:
                            w += [(VV, proj_done_v), (SS, proj_done_s)]
                        sc_t[hh] = p.add("tensor", mm(
                            pSC[:, hh, 0:kwid],
                            qf[64 * (h % 2):64 * (h % 2) + 64, h // 2, w4 * 128:(w4 + 1) * 128],
                            kf[64 * (h % 2):64 * (h % 2) + 64, h // 2, kcol:kcol + kwid],
                            True, True), waits=w, inc=(TT, 1))
                    for hh in range(2):
                        pSC_free[hh] = vop(lambda v, b2=hh, m2=mvar, kw=kwid: v.tensor_tensor(
                            amsk[:, b2, 0:kw], pSC[:, b2, 0:kw], m2[:, 0:kw], ALU.add),
                            waits=[(TT, sc_t[hh])] + ([(SS, prev_exp[hh])] if prev_exp.get(hh) else []))
                    for hh in range(2):
                        vop(lambda v, b2=hh, kw=kwid: v.reduce_max(
                            asml[:, b2, 0:1], amsk[:, b2, 0:kw], axis=AX.X, negate=True))
                    for hh in range(2):
                        vop(lambda v, b2=hh: v.tensor_scalar(
                            asml[:, b2, 1:2], asml[:, b2, 0:1], 0.125, None, ALU.mult))
                    vw = p.val(VV)
                    exp_s = {}
                    for hh in range(2):
                        exp_s[hh] = sop(lambda s, b2=hh, kw=kwid: s.activation(
                            aexp[:, b2, 0:kw], amsk[:, b2, 0:kw], AF.Exp,
                            bias=asml[:, b2, 1:2], scale=cvals[:, 2:3],
                            accum_out=asml[:, b2, 2:3]),
                            waits=[(VV, vw)] + ([(SS, prev_prb[hh])] if prev_prb.get(hh) else []))
                    prev_exp = dict(exp_s)
                    for hh in range(2):
                        vop(lambda v, b2=hh: v.reciprocal(asml[:, b2, 3:4], asml[:, b2, 2:3]),
                            waits=[(SS, exp_s[hh])])
                    vw2 = p.val(VV)
                    prb_s = {}
                    for hh in range(2):
                        prb_s[hh] = sop(lambda s, b2=hh, kw=kwid, pb2=pb4: s.activation(
                            aprb[:, pb2, b2, 0:kw], aexp[:, b2, 0:kw], AF.Identity,
                            bias=0.0, scale=asml[:, b2, 3:4]),
                            waits=[(VV, vw2), (TT, tp_par.get(pb4, 0))])
                    prev_prb = dict(prb_s)
                    if pend is not None:
                        emit_tpav(pend)
                    pend = (w4, gw, kwid, hp, dict(prb_s), pb4)
            if pend is not None:
                emit_tpav(pend)
                pend = None
            win_done_v = p.val(VV) if P4PART >= 2 else p.val(VV)
            for tt4 in range(4 if P4PART >= 3 else 0):
                for half in range(2):
                    j = tt4 * 2 + half
                    bb = j % 2
                    for icc in range(8):
                        w = []
                        if j == 0 and icc == 0:
                            w = [(VV, win_done_v)]
                        if icc == 0 and j >= 2:
                            w.append((VV, oev[j - 2]))
                        p.add("tensor", mm(pQK[:, bb, :], apre[:, icc, tt4 * 128:(tt4 + 1) * 128],
                                           wat[:, 3, icc, half * 512:(half + 1) * 512],
                                           icc == 0, icc == 7),
                              waits=w, inc=((TT, 1) if icc == 7 else None))
                    if j == 0:
                        oev = {}
                    oev[j] = vop(lambda v, b2=bb, h2=half: v.tensor_tensor(
                        aout1[:, b2, :], pQK[:, b2, :], bob[:, h2 * 512:(h2 + 1) * 512], ALU.add),
                        waits=[(TT, p.val(TT))] + ([(ST, ost_free[bb])] if bb in ost_free else []))
                    ost_free[bb] = dma("sync", d_attn[t0 + tt4 * 128:t0 + (tt4 + 1) * 128,
                                                      half * 512:(half + 1) * 512],
                                       aout1[:, bb, :], waits=[(VV, oev[j])], s=ST)
            mc_ttdone[mc] = p.val(TT)
            mc_ssdone = p.val(SS)
            mc_vvdone = p.val(VV)
        p4_done = p.val(VV)
        p4_tt = p.val(TT)
        p4_st = p.val(ST)
        es4.close()
        if debug >= 1 and NMC == 8:
            o = dbgten("g_attn", [S, D])
            p.add("sync", lambda e, oo=o: e.dma_start(out=oo[:, :], in_=d_attn[:, :]),
                  waits=[(ST, p4_st)], inc=(ST, 16))
            o = dbgten("g_xn", [S, D], BF16)
            p.add("sync", lambda e, oo=o: e.dma_start(out=oo[:, :], in_=d_xn[:, :]),
                  waits=[(ST, p4_st)], inc=(ST, 16))


        if MAXPH >= 5:
            xn5 = sba("xn5", [128, 2, D], BF16, RC)
            xnb5 = sba("xnb5", [128, 4, 128], BF16, RC + 8192)
            mf5 = sba("mf5", [128, 2, 512], BF16, RC + 12288)
            mo5 = sba("mo5", [128, 2, 2, 512], BF16, RC + 16384)
            me5 = sba("me5", [128, 4, 512], F32, RC + 20480)
            cvb5 = sba("cvb5", [128, 2, 2], F32, RC + 28672)
            stg5 = sba("stg5", [128, 2, 8, 64], BF16, RC + 28736)
            sto5 = sba("sto5", [128, 2, 4, 128], BF16, RC + 30784)
            mvb5 = sba("mvb5", [128, 4, 512], BF16, RC + 34880)
            es5 = ExitStack()
            pF5 = es5.enter_context(nc.psum_tensor("pF5", [128, 2, 512], F32))
            pO5 = es5.enter_context(nc.psum_tensor("pO5", [128, 2, 512], F32))
            d_mov = dt("d_mov", [NTILE, 128, 1024], BF16, kind="Internal").ap()
            mb_done = {}
            fl_st = {}
            fl_ev = {}
            for Ti in range(NTILE):
                b = Ti % 2
                w = [(ST, p4_st), (VV, p4_done)] if Ti == 0 else []
                if Ti >= 2:
                    w.append((VV, mb_done[Ti - 2]))
                t5_ld = dma("sync", xn5[:, b, :], d_xn[Ti * 128:(Ti + 1) * 128, :], waits=w)
                vv_w = [(VV, p4_done)] if Ti == 0 else []
                for k in range(2):
                    vop(lambda v, kk=k, tt_=Ti: v.tensor_scalar(
                        me5[:, kk, :], io512[:, :], colf[:, kk, tt_:tt_ + 1], None, ALU.is_equal),
                        waits=(vv_w if k == 0 else ()))
                for k in range(2):
                    vop(lambda v, kk=k, tt_=Ti: v.tensor_scalar(
                        cvb5[:, kk, 0:1], colov[:, kk, tt_:tt_ + 1], -512.0, None, ALU.add))
                vop(lambda v, bb=b: v.tensor_tensor(mf5[:, bb, :], me5[:, 0, :], me5[:, 1, :], ALU.add))
                for k in range(2):
                    vop(lambda v, kk=k, tt_=Ti: v.tensor_scalar(
                        me5[:, kk, :], io512[:, :], colov[:, kk, tt_:tt_ + 1], None, ALU.is_equal))
                for k in range(2):
                    vop(lambda v, kk=k: v.tensor_scalar(
                        me5[:, 2 + kk, :], io512[:, :], cvb5[:, kk, 0:1], None, ALU.is_equal))
                vop(lambda v, bb=b: v.tensor_tensor(mo5[:, bb, 0, :], me5[:, 0, :], me5[:, 1, :], ALU.add))
                mb_done[Ti] = vop(lambda v, bb=b: v.tensor_tensor(mo5[:, bb, 1, :], me5[:, 2, :], me5[:, 3, :], ALU.add))
                dma("sync", d_mov[Ti, :, :], mo5[:, b, :, :].rearrange("p a b -> p (a b)"),
                    waits=[(VV, mb_done[Ti])], s=ST)
                for ic in range(8):
                    bb2 = ic % 2
                    gkey = Ti * 8 + ic
                    wfl = [(LD, t5_ld), (VV, mb_done[Ti])] if ic == 0 else []
                    if gkey >= 2:
                        wfl = list(wfl) + [(VV, fl_ev[gkey - 2])]
                    p.add("tensor", mm(pF5[:, bb2, :], xn5[:, b, ic * 128:(ic + 1) * 128], mf5[:, b, :],
                                       True, True), waits=wfl, inc=(TT, 1))
                    wv = [(TT, p.val(TT))]
                    if gkey >= 2:
                        wv.append((ST, fl_st[gkey - 2]))
                    fl_ev[gkey] = vop(lambda v, b3=bb2: v.tensor_copy(
                        stg5[:, b3, :, :].rearrange("p a b -> p (a b)"), pF5[:, b3, :]), waits=wv)
                    fl_st[gkey] = dma("sync",
                                      d_buf[:, ic, :, Ti * W:(Ti + 1) * W].rearrange("e p j -> p e j"),
                                      stg5[:, bb2, :, :], waits=[(VV, fl_ev[gkey])], s=ST)
            passA_st = p.val(ST)
            passA_vv = p.val(VV)
            ov_ev = {}
            ov_st = {}
            ldB = {}
            for g in range(16):
                ic, half = g // 2, g % 2
                hb = g % 2
                for Ti in range(NTILE):
                    tb = Ti % 4
                    wl = [(ST, passA_st)] if (g == 0 and Ti == 0) else []
                    if (g * NTILE + Ti) >= 4:
                        wl.append((TT, ldB[(g * NTILE + Ti) - 4]))
                    l1 = dma("sync", xnb5[:, tb, :], d_xn[Ti * 128:(Ti + 1) * 128, ic * 128:(ic + 1) * 128],
                             waits=wl)
                    l2 = dma("sync", mvb5[:, tb, :], d_mov[Ti, :, half * 512:(half + 1) * 512])
                    w3 = [(LD, l2)]
                    if Ti == 0 and g >= 2:
                        w3.append((VV, ov_ev[g - 2]))
                    ldB[g * NTILE + Ti] = p.add("tensor", mm(pO5[:, hb, :], xnb5[:, tb, :], mvb5[:, tb, :],
                                                             Ti == 0, Ti == NTILE - 1),
                                                waits=w3, inc=(TT, 1))
                wv = [(TT, p.val(TT))]
                if g >= 2:
                    wv.append((ST, ov_st[g - 2]))
                ov_ev[g] = vop(lambda v, h3=hb: v.tensor_copy(
                    sto5[:, h3, :, :].rearrange("p a b -> p (a b)"), pO5[:, h3, :]), waits=wv)
                ov_st[g] = dma("sync",
                               d_buf[half * 4:(half + 1) * 4, ic, :, NTILE * W:SLOT_E].rearrange("e p j -> p e j"),
                               sto5[:, hb, :, :], waits=[(VV, ov_ev[g])], s=ST)
            p5_st = p.val(ST)
            p5_tt = p.val(TT)
            es5.close()

        if MAXPH >= 6:
            STS = [384, 384, 384, 384, 384, 256]
            NST = len(STS)
            SOFFS = [sum(STS[:i]) for i in range(NST)]
            TAIL = RD + 65536
            assert TAIL + 6144 <= 229344, TAIL
            w1r = sba("w1r", [128, 8, F4], BF16, RB)
            w2r = sba("w2r", [128, 32, D], BF16, RD)
            h6 = sba("h6", [128, 32, 384], BF16, RC)
            bst6 = sba("bst6", [128, 2, 8, 384], BF16, RC + 24576)
            ysb = sba("ysb", [128, 2, D], BF16, RC + 46080)
            g16 = sba("g16", [128, 4, 384], F32, TAIL)
            es6 = ExitStack()
            pH6 = es6.enter_context(nc.psum_tensor("pH6", [128, 4, 512], F32))
            pY6 = es6.enter_context(nc.psum_tensor("pY6", [128, 2, 1024], F32))
            w1_ldg = {}
            w2_ldg = {}
            w1_lds = {}
            w2_lds = {}
            pcast = {}
            for e9 in range(2):
                dma("gpsimd", d_w1b[e9, :, :], w1[e9, :, :], s=LDG,
                    waits=([(TT, p4_tt)] if e9 == 0 else ()))
                dma("gpsimd", d_w2b[e9, :, :], w2[e9, :, :], s=LDG)
                pcast[e9] = p.val(LDG)
            for icc in range(8):
                dma("gpsimd", w1r[:, icc, :],
                    d_w1b[0, icc * 128:(icc + 1) * 128, :],
                    s=LDG, waits=([(VV, p4_done)] if icc == 0 else ()))
            w1_ldg[0] = p.val(LDG)
            for fg in range(8):
                dma("gpsimd", w2r[:, fg * 4:(fg + 1) * 4, :],
                    d_w2b[0, fg * 512:(fg + 1) * 512, :].rearrange("(c p) o -> p c o", p=128),
                    s=LDG, waits=([(VV, p2_done)] if fg == 0 else ()))
            w2_ldg[0] = p.val(LDG)
            scg = 0
            gfi = 0
            nq = na = nh = 0
            ye_ev = {}
            ysb_st = {}
            h_done = {}
            bst_ldg = {}
            bst_used = {}
            ye_last_mm = {-1: 0}
            gmeta = {}
            sq_v = {}
            q6_v = {}
            ar_v = {}
            sg_v = {}

            def gelu_sq(g):
                hb, w5, tt, _ = gmeta[g]
                sq_v[g] = sop(lambda s, h3=hb, w6=w5: s.activation(
                    g16[:, h3, 0:w6], pH6[:, h3, 0:w6], AF.Square), waits=[(TT, tt)])

            def gelu_q6(g):
                hb, w5, _, _ = gmeta[g]
                q6_v[g] = gop(lambda gg, h3=hb, w6=w5: gg.tensor_scalar(
                    g16[:, h3, 0:w6], g16[:, h3, 0:w6], GELU_C, 1.0, ALU.mult, ALU.add),
                    waits=[(SS, sq_v[g])])

            def gelu_ar(g):
                hb, w5, _, _ = gmeta[g]
                ar_v[g] = vop(lambda v, h3=hb, w6=w5: v.tensor_tensor(
                    g16[:, h3, 0:w6], g16[:, h3, 0:w6], pH6[:, h3, 0:w6], ALU.mult),
                    waits=[(PP, q6_v[g])])

            def gelu_sg(g):
                hb, w5, _, _ = gmeta[g]
                sg_v[g] = sop(lambda s, h3=hb, w6=w5: s.activation(
                    g16[:, h3, 0:w6], g16[:, h3, 0:w6], AF.Sigmoid, bias=0.0, scale=cvals[:, 5:6]),
                    waits=[(VV, ar_v[g])])

            def gelu_hev(g):
                hb, w5, _, f5 = gmeta[g]
                h_done[g] = vop(lambda v, h3=hb, f6=f5, w6=w5: v.tensor_tensor(
                    h6[:, f6, 0:w6], g16[:, h3, 0:w6], pH6[:, h3, 0:w6], ALU.mult),
                    waits=[(SS, sg_v[g])])

            def gelu_pump():
                nonlocal nq, na, nh
                if nq <= gfi - 1:
                    gelu_q6(nq)
                    nq += 1
                if na <= gfi - 2:
                    gelu_ar(na)
                    gelu_sg(na)
                    na += 1
                if nh <= gfi - 3:
                    gelu_hev(nh)
                    nh += 1

            def gelu_flush():
                nonlocal nq, na, nh
                while nq <= gfi:
                    gelu_q6(nq)
                    nq += 1
                while na <= gfi:
                    gelu_ar(na)
                    gelu_sg(na)
                    na += 1
                while nh <= gfi:
                    gelu_hev(nh)
                    nh += 1

            for e_i in range(E):
                lv = dma("sync", rowst[:, :], b2row[:, e_i * D:(e_i + 1) * D],
                         waits=[(TT, ye_last_mm[e_i - 1])] + ([(ST, p5_st)] if e_i == 0 else []))
                for sti in range(NST):
                    stw = STS[sti]
                    soff = SOFFS[sti]
                    sgl = e_i * NST + sti
                    sb6 = sgl % 2
                    wb = [(ST, p5_st)] if sgl == 0 else []
                    if bst_used.get(sb6):
                        wb.append((TT, bst_used[sb6]))
                    bst_ldg[sgl] = dma(
                        "sync", bst6[:, sb6, :, 0:stw],
                        d_buf.rearrange("e c p j -> p e c j")[:, e_i, :, soff:soff + stw],
                        s=LD, waits=wb)
                    if e_i + 2 < E and sti < 4:
                        dma("gpsimd", d_w1b[e_i + 2, sti * 256:(sti + 1) * 256, :],
                            w1[e_i + 2, sti * 256:(sti + 1) * 256, :], s=LDG)
                        dma("gpsimd", d_w2b[e_i + 2, sti * 1024:(sti + 1) * 1024, :],
                            w2[e_i + 2, sti * 1024:(sti + 1) * 1024, :], s=LDG)
                        if sti == 3:
                            pcast[e_i + 2] = p.val(LDG)
                    lst_f4tt = {}
                    for f4 in range(32):
                        hb = gfi % 4
                        for icc in range(8):
                            w4_ = []
                            if icc == 0:
                                w4_ = [(LDG, w1_ldg[e_i]), (LD, bst_ldg[sgl])]
                                if w1_lds.get(e_i):
                                    w4_.append((LD, w1_lds[e_i]))
                                if h_done.get(gfi - 4):
                                    w4_.append((VV, h_done[gfi - 4]))
                            p.add("tensor", mm(pH6[:, hb, 0:stw], w1r[:, icc, f4 * 128:(f4 + 1) * 128],
                                               bst6[:, sb6, icc, 0:stw], icc == 0, icc == 7),
                                  waits=w4_, inc=((TT, 1) if icc == 7 else None))
                        h_mm = p.val(TT)
                        lst_f4tt[f4] = h_mm
                        gmeta[gfi] = (hb, stw, h_mm, f4)
                        gelu_sq(gfi)
                        gelu_pump()
                        gfi += 1
                    gfi -= 1
                    gelu_flush()
                    gfi += 1
                    h_all = p.val(VV)
                    bst_used[sb6] = p.val(TT)
                    if sti == NST - 1 and e_i + 1 < E:
                        for fh in range(2):
                            for icc in range(8):
                                dma("gpsimd" if icc == 0 else "sync",
                                    w1r[:, icc, fh * 2048:(fh + 1) * 2048],
                                    d_w1b[e_i + 1, icc * 128:(icc + 1) * 128, fh * 2048:(fh + 1) * 2048],
                                    s=(LDG if icc == 0 else LD),
                                    waits=(([(TT, lst_f4tt[16 * fh + 15])] +
                                            ([(LDG, pcast[e_i + 1])] if fh == 0 else []))
                                           if icc <= 1 else ()))
                        w1_ldg[e_i + 1] = p.val(LDG)
                        w1_lds[e_i + 1] = p.val(LD)
                    nsc = stw // 128
                    for sc in range(nsc):
                        yb = scg % 2
                        for half in range(2):
                            wy = [(LD, lv)]
                            if half == 0:
                                wy = [(VV, h_all), (LDG, w2_ldg[e_i]), (LD, lv)]
                                if w2_lds.get(e_i):
                                    wy.append((LD, w2_lds[e_i]))
                                if ye_ev.get(scg - 2):
                                    wy.append((VV, ye_ev[scg - 2]))
                            p.add("tensor", mm(pY6[:, yb, half * 512:(half + 1) * 512],
                                               ones1f[0:1, :],
                                               rowst[0:1, half * 512:(half + 1) * 512],
                                               True, False), waits=wy, inc=None)
                            for f4 in range(32):
                                p.add("tensor", mm(pY6[:, yb, half * 512:(half + 1) * 512],
                                                   h6[:, f4, sc * 128:(sc + 1) * 128],
                                                   w2r[:, f4, half * 512:(half + 1) * 512],
                                                   False, f4 == 31),
                                      inc=((TT, 1) if (f4 == 31 and half == 1) else None))
                        ye_mm = p.val(TT)
                        for half in range(2):
                            wv = [(TT, ye_mm)] if half == 0 else []
                            if half == 0 and ysb_st.get(scg - 2):
                                wv.append((ST, ysb_st[scg - 2]))
                            vop(lambda v, y3=yb, h3=half: v.tensor_copy(
                                ysb[:, y3, h3 * 512:(h3 + 1) * 512],
                                pY6[:, y3, h3 * 512:(h3 + 1) * 512]), waits=wv)
                        ye_ev[scg] = p.val(VV)
                        gsc = soff // 128 + sc
                        ysb_st[scg] = dma("sync", d_ye[e_i, gsc, :, :], ysb[:, yb, :],
                                          waits=[(VV, ye_ev[scg])], s=ST)
                        scg += 1
                ye_last_mm[e_i] = p.val(TT)
                if e_i + 1 < E:
                    for fg in range(8):
                        dma("gpsimd" if fg == 0 else "sync",
                            w2r[:, fg * 4:(fg + 1) * 4, :],
                            d_w2b[e_i + 1, fg * 512:(fg + 1) * 512, :].rearrange("(c p) o -> p c o", p=128),
                            s=(LDG if fg == 0 else LD),
                            waits=(([(TT, ye_last_mm[e_i]), (LDG, pcast[e_i + 1])] if fg <= 1 else ())))
                    w2_ldg[e_i + 1] = p.val(LDG)
                    w2_lds[e_i + 1] = p.val(LD)
            p6_st = p.val(ST)
            p6_vv = p.val(VV)
            p6_tt = p.val(TT)
            es6.close()

        if MAXPH >= 7:
            yeov = sba("yeov", [128, 8, D], BF16, RD)
            ystk = sba("ystk", [128, 2, 4, D], BF16, RD + 16384)
            mg7 = sba("mg7", [128, 512], BF16, RC)
            mgo7 = sba("mgo7", [128, 2, 512], BF16, RC + 1024)
            me7 = sba("me7", [128, 4, 512], F32, RC + 3072)
            mgt7 = sba("mgt7", [128, 4, 128], BF16, RC + 11264)
            mgo7t = sba("mgo7t", [128, 8, 128], BF16, RC + 12288)
            cvb7 = sba("cvb7", [128, 2, 1], F32, RC + 14336)
            xld7 = sba("xld7", [128, 2, D], F32, RC + 16384)
            atl7 = sba("atl7", [128, 2, D], F32, RC + 24576)
            out7 = sba("out7", [128, 2, D], F32, RC + 32768)
            es7 = ExitStack()
            pMG = es7.enter_context(nc.psum_tensor("pMG", [128, 2, 1024], BF16))
            pFF = es7.enter_context(nc.psum_tensor("pFF", [128, 2, 1024], F32))
            if P7PART >= 1:
                ld_ov = dma("sync", yeov[:, :, :],
                            d_ye[:, NSC - 1, :, :].rearrange("e p o -> p e o"), waits=[(ST, p6_st)])
            out_stv = {}
            cmb_ev = {}
            mgt_ev = {}
            ld_yp = {}
            ff_mm = {}
            tpz_d = {}
            for Ti in range(NTILE):
                b = Ti % 2
                ch = (Ti * W) // 128
                pb = Ti % 2
                ro = 64 * (Ti % 2)
                ypw = [(ST, p6_st)] if Ti == 0 else []
                if ff_mm.get(Ti - 2):
                    ypw.append((TT, ff_mm[Ti - 2]))
                for j in range(4 if P7PART >= 1 else 0):
                    for s2 in range(2):
                        ld_yp[Ti] = dma("sync", ystk[64 * s2:64 * s2 + 64, b, j, :],
                                        d_ye[2 * j + s2, ch, ro:ro + 64, :],
                                        waits=(ypw if (j == 0 and s2 == 0) else ()))
                vvw = [(VV, p6_vv), (TT, p6_tt)] if Ti == 0 else []
                if P7PART >= 2:
                    if mgt_ev.get(Ti - 1):
                        vvw = list(vvw) + [(TT, tpz_d.get(Ti - 1, 0))]
                    for k in range(2):
                        vop(lambda v, kk=k, tt_=Ti: v.tensor_scalar(
                            me7[:, kk, :], io512[:, :], colf[:, kk, tt_:tt_ + 1], None, ALU.is_equal),
                            waits=(vvw if k == 0 else ()))
                    for k in range(2):
                        vop(lambda v, kk=k, tt_=Ti: v.tensor_scalar(
                            me7[:, kk, :], me7[:, kk, :], gsel[:, kk, tt_:tt_ + 1], None, ALU.mult))
                    for k in range(2):
                        vop(lambda v, kk=k, tt_=Ti: v.tensor_scalar(
                            cvb7[:, kk, 0:1], colov[:, kk, tt_:tt_ + 1], -512.0, None, ALU.add))
                    v_mg = vop(lambda v: v.tensor_tensor(mg7[:, :], me7[:, 0, :], me7[:, 1, :], ALU.add))
                    for k in range(2):
                        vop(lambda v, kk=k, tt_=Ti: v.tensor_scalar(
                            me7[:, kk, :], io512[:, :], colov[:, kk, tt_:tt_ + 1], None, ALU.is_equal))
                    for k in range(2):
                        vop(lambda v, kk=k: v.tensor_scalar(
                            me7[:, 2 + kk, :], io512[:, :], cvb7[:, kk, 0:1], None, ALU.is_equal))
                    for k in range(2):
                        vop(lambda v, kk=k, tt_=Ti: v.tensor_scalar(
                            me7[:, kk, :], me7[:, kk, :], gsel[:, kk, tt_:tt_ + 1], None, ALU.mult))
                    for k in range(2):
                        vop(lambda v, kk=k, tt_=Ti: v.tensor_scalar(
                            me7[:, 2 + kk, :], me7[:, 2 + kk, :], gsel[:, kk, tt_:tt_ + 1], None, ALU.mult))
                    v_mo1 = vop(lambda v: v.tensor_tensor(mgo7[:, 0, :], me7[:, 0, :], me7[:, 1, :], ALU.add))
                    v_mo2 = vop(lambda v: v.tensor_tensor(mgo7[:, 1, :], me7[:, 2, :], me7[:, 3, :], ALU.add))
                if P7PART >= 3:
                    tw0 = [(VV, v_mo2)]
                    if mgt_ev.get(Ti - 1):
                        tw0.append((VV, mgt_ev[Ti - 1]))
                    for j in range(4):
                        p.add("tensor", lambda t, jj=j: t.transpose(
                            pMG[:, 0, jj * 128:(jj + 1) * 128], mg7[:, jj * 128:(jj + 1) * 128], identb[:, :]),
                            waits=(tw0 if j == 0 else ()), inc=((TT, 1) if j == 3 else None))
                    tpf = p.val(TT)
                    for hh in range(2):
                        for j in range(4):
                            p.add("tensor", lambda t, h3=hh, jj=j: t.transpose(
                                pMG[:, 1, (h3 * 4 + jj) * 128:(h3 * 4 + jj + 1) * 128],
                                mgo7[:, h3, jj * 128:(jj + 1) * 128], identb[:, :]),
                                inc=((TT, 1) if (hh == 1 and j == 3) else None))
                    tpz = p.val(TT)
                    tpz_d[Ti] = tpz
                    ev1 = vop(lambda v: v.tensor_copy(mgt7.rearrange("p a b -> p (a b)"), pMG[:, 0, 0:512]),
                              waits=[(TT, tpf)])
                    mgt_ev[Ti] = vop(lambda v: v.tensor_copy(mgo7t.rearrange("p a b -> p (a b)"), pMG[:, 1, :]),
                                     waits=[(TT, tpz)])
                for half in range(2 if P7PART >= 4 else 0):
                    first = True
                    for j in range(4):
                        w = []
                        if first:
                            w = [(VV, mgt_ev[Ti]), (LD, ld_yp[Ti]), (LD, ld_ov)]
                            if cmb_ev.get(Ti - 2):
                                w.append((VV, cmb_ev[Ti - 2]))
                        p.add("tensor", mm(pFF[:, pb, half * 512:(half + 1) * 512],
                                           mgt7[:, j, :],
                                           ystk[:, b, j, half * 512:(half + 1) * 512],
                                           first, False), waits=w, inc=None)
                        first = False
                    for e_i in range(E):
                        p.add("tensor", mm(pFF[:, pb, half * 512:(half + 1) * 512],
                                           mgo7t[:, e_i, :],
                                           yeov[:, e_i, half * 512:(half + 1) * 512],
                                           False, e_i == E - 1),
                              inc=((TT, 1) if e_i == E - 1 else None))
                ff_mm[Ti] = p.val(TT)
                lxw = [(ST, p6_st), (VV, p6_vv), (TT, p6_tt)] if Ti == 0 else []
                if cmb_ev.get(Ti - 2):
                    lxw = list(lxw) + [(VV, cmb_ev[Ti - 2])]
                lx = dma("sync", xld7[:, b, :], x_sh[Ti * 128:(Ti + 1) * 128, :], waits=lxw)
                la = dma("sync", atl7[:, b, :], d_attn[Ti * 128:(Ti + 1) * 128, :])
                cw = [(TT, ff_mm.get(Ti, 0)), (LD, la)]
                if out_stv.get(Ti - 2):
                    cw.append((ST, out_stv[Ti - 2]))
                for half in range(2):
                    if P7PART >= 4:
                        vop(lambda v, b3=b, h3=half: v.tensor_tensor(
                            out7[:, b3, h3 * 512:(h3 + 1) * 512], pFF[:, b3, h3 * 512:(h3 + 1) * 512],
                            xld7[:, b3, h3 * 512:(h3 + 1) * 512], ALU.add), waits=(cw if half == 0 else ()))
                    else:
                        vop(lambda v, b3=b, h3=half: v.tensor_copy(
                            out7[:, b3, h3 * 512:(h3 + 1) * 512],
                            xld7[:, b3, h3 * 512:(h3 + 1) * 512]), waits=(cw if half == 0 else ()))
                for half in range(2):
                    vop(lambda v, b3=b, h3=half: v.tensor_tensor(
                        out7[:, b3, h3 * 512:(h3 + 1) * 512], out7[:, b3, h3 * 512:(h3 + 1) * 512],
                        atl7[:, b3, h3 * 512:(h3 + 1) * 512], ALU.add))
                cmb_ev[Ti] = p.val(VV)
                out_stv[Ti] = dma("sync", y[Ti * 128:(Ti + 1) * 128, :], out7[:, b, :],
                                  waits=[(VV, cmb_ev[Ti])], s=ST)
            p7_st = p.val(ST)

        p.add("sync", lambda e: e.nop(),
              waits=[(ST, p.val(ST)), (VV, p.val(VV)), (SS, p.val(SS)), (TT, p.val(TT)),
                     (LDG, p.val(LDG)), (LD, p.val(LD)), (PP, p.val(PP))])
        block = es.enter_context(nc.Block())
        p.emit(block)
    return nc, dbg


def host_inputs(inputs, core):
    x = np.asarray(inputs["L0_attn_x"]).reshape(T, D)
    xr = np.concatenate([x[core * S:], x[:core * S]], axis=0)
    io512 = np.tile(np.arange(512, dtype=np.float32), (128, 1))
    wrap = np.zeros((1, NCOL), np.float32)
    nwrap = (core * S) // 128
    if nwrap:
        wrap[0, NCOL - nwrap:] = 1.0
    return {
        "x_sh": np.ascontiguousarray(x[core * S:(core + 1) * S]),
        "xfm": np.ascontiguousarray(xr.T),
        "wq": np.ascontiguousarray(inputs["L0_attn_Wq"]), "wk": np.ascontiguousarray(inputs["L0_attn_Wk"]),
        "wv": np.ascontiguousarray(inputs["L0_attn_Wv"]), "wo": np.ascontiguousarray(inputs["L0_attn_Wo"]),
        "bqr": np.ascontiguousarray(np.asarray(inputs["L0_attn_bq"]).reshape(8, 128).T),
        "bkr": np.ascontiguousarray(np.asarray(inputs["L0_attn_bk"]).reshape(8, 128).T),
        "bvrow": np.ascontiguousarray(np.asarray(inputs["L0_attn_bv"]).reshape(1, D)),
        "borow": np.ascontiguousarray(np.asarray(inputs["L0_attn_bo"]).reshape(1, D)),
        "wgi": np.ascontiguousarray(inputs["L0_ffn_Wg"]),
        "bgc": np.ascontiguousarray(np.asarray(inputs["L0_ffn_bg"]).reshape(8, 1)),
        "w1": np.ascontiguousarray(inputs["L0_ffn_W1"]),
        "b1r": np.ascontiguousarray(np.asarray(inputs["L0_ffn_b1"]).reshape(8, 32, 128).transpose(2, 1, 0)),
        "w2": np.ascontiguousarray(inputs["L0_ffn_W2"]),
        "b2row": np.ascontiguousarray(np.asarray(inputs["L0_ffn_b2"]).reshape(1, E * D)),
        "c_lt": np.triu(np.ones((128, 128), np.float32), 1),
        "c_io512": io512,
        "c_io512b": io512 + 512.0,
        "c_io8": np.tile(np.arange(8, dtype=np.float32), (128, 1)),
        "c_identf": np.eye(128, dtype=np.float32),
        "c_identb": np.eye(128).astype(ml_dtypes.bfloat16),
        "c_mask": np.where(
            (np.arange(256)[None, :] < 128) | ((np.arange(256)[None, :] - 128) <= np.arange(128)[:, None]),
            0.0, -1e30).astype(np.float32),
        "c_mask0": np.where(np.arange(128)[None, :] <= np.arange(128)[:, None], 0.0, -1e30).astype(np.float32),
        "wrapm": wrap,
        "c_vals": np.tile(np.array([1e-5, -0.5, 0.125, 0.5, GELU_C, GELU_S, 1.0, 0.0], np.float32), (128, 1)),
    }


def kernel(**inputs):
    nc, _ = build(debug=0)
    ims = [host_inputs(inputs, c) for c in range(NCORE)]
    res = run_bass_kernel_spmd(nc, ims, core_ids=list(range(NCORE)))
    out = np.stack([np.asarray(res.results[c]["y"]) for c in range(NCORE)], axis=0)
    return out.astype(np.float32)


# version 11
# speedup vs baseline: 1.0892x; 1.0892x over previous
import os
import numpy as np
import ml_dtypes
from contextlib import ExitStack
import concourse.bass as bass
import concourse.mybir as mybir
from concourse.bass_utils import run_bass_kernel_spmd

F32 = mybir.dt.float32
BF16 = mybir.dt.bfloat16
AF = mybir.ActivationFunctionType
ALU = mybir.AluOpType
AX = mybir.AxisListType

NCORE = 8
D = 1024
S = 4096
T = 32768
NTILE = 32
NCOL = 256
E = 8
F4 = 4096
CAP = 9831
W = 64
OV = 128
SLOT_E = NTILE * W + OV
NSC = SLOT_E // 128
NH = 16
HD = 64
BIG = 1.0e9
GELU_C = 0.044715
GELU_S = 1.5957691216057308

ENGINES = ("sync", "gpsimd", "vector", "scalar", "tensor")


class Prog:
    def __init__(self):
        self.ops = {e: [] for e in ENGINES}
        self.cnt = {}

    def add(self, eng, fn, waits=(), inc=None):
        waits = tuple((s, v) for (s, v) in waits if v > 0)
        self.ops[eng].append((fn, waits, inc))
        if inc is not None:
            s, n = inc
            self.cnt[s] = self.cnt.get(s, 0) + n
            return self.cnt[s]
        return None

    def val(self, s):
        return self.cnt.get(s, 0)

    def emit(self, block):
        for ename in ENGINES:
            lst = self.ops[ename]
            if not lst:
                continue

            def body(eng, lst=lst):
                for fn, waits, inc in lst:
                    for (s, v) in waits:
                        eng.wait_ge(s, v)
                    ins = fn(eng)
                    if inc is not None:
                        ins.then_inc(inc[0], inc[1])

            getattr(block, ename)(body)


def build(debug=0):
    nc = bass.Bass("TRN2", target_bir_lowering=False)
    dt = nc.dram_tensor

    x_sh = dt("x_sh", [S, D], F32, kind="ExternalInput").ap()
    xfm = dt("xfm", [D, T], F32, kind="ExternalInput").ap()
    wq = dt("wq", [D, D], F32, kind="ExternalInput").ap()
    wk = dt("wk", [D, D], F32, kind="ExternalInput").ap()
    wv = dt("wv", [D, D], F32, kind="ExternalInput").ap()
    wo = dt("wo", [D, D], F32, kind="ExternalInput").ap()
    bqr = dt("bqr", [128, 8], F32, kind="ExternalInput").ap()
    bkr = dt("bkr", [128, 8], F32, kind="ExternalInput").ap()
    bvrow = dt("bvrow", [1, D], F32, kind="ExternalInput").ap()
    borow = dt("borow", [1, D], F32, kind="ExternalInput").ap()
    wgi = dt("wgi", [D, E], F32, kind="ExternalInput").ap()
    bgc = dt("bgc", [8, 1], F32, kind="ExternalInput").ap()
    w1 = dt("w1", [E, D, F4], F32, kind="ExternalInput").ap()
    b1r = dt("b1r", [128, 32, 8], F32, kind="ExternalInput").ap()
    w2 = dt("w2", [E, F4, D], F32, kind="ExternalInput").ap()
    b2row = dt("b2row", [1, E * D], F32, kind="ExternalInput").ap()
    c_lt = dt("c_lt", [128, 128], F32, kind="ExternalInput").ap()
    c_io512 = dt("c_io512", [128, 512], F32, kind="ExternalInput").ap()
    c_io512b = dt("c_io512b", [128, 512], F32, kind="ExternalInput").ap()
    c_io8 = dt("c_io8", [128, 8], F32, kind="ExternalInput").ap()
    c_identf = dt("c_identf", [128, 128], F32, kind="ExternalInput").ap()
    c_identb = dt("c_identb", [128, 128], BF16, kind="ExternalInput").ap()
    c_mask = dt("c_mask", [128, 256], F32, kind="ExternalInput").ap()
    c_mask0 = dt("c_mask0", [128, 128], F32, kind="ExternalInput").ap()
    wrapm = dt("wrapm", [1, NCOL], F32, kind="ExternalInput").ap()
    c_vals = dt("c_vals", [128, 8], F32, kind="ExternalInput").ap()

    y = dt("y", [S, D], F32, kind="ExternalOutput").ap()
    d_xn = dt("d_xn", [S, D], BF16, kind="Internal").ap()
    d_attn = dt("d_attn", [S, D], F32, kind="Internal").ap()
    d_buf = dt("d_buf", [E, 8, 128, SLOT_E], BF16, kind="Internal").ap()
    d_w1b = dt("d_w1b", [E, D, F4], BF16, kind="Internal").ap()
    d_w2b = dt("d_w2b", [E, F4, D], BF16, kind="Internal").ap()
    d_ye = dt("d_ye", [E, NSC, 128, D], BF16, kind="Internal").ap()

    dbg = {}

    def dbgten(nm, shp, dd=F32):
        dbg[nm] = dt(nm, shp, dd, kind="ExternalOutput").ap()
        return dbg[nm]

    p = Prog()
    es = ExitStack()
    sbt = lambda n, s, d=F32: es.enter_context(nc.sbuf_tensor(n, s, d))
    sem = lambda n: es.enter_context(nc.semaphore(n))

    def sba(name, shape, dtyp, off):
        return nc.alloc_sbuf_tensor_at(name, shape, dtyp, offset=off).ap()

    with es:
        g9 = sbt("g9", [128, 8, 9], BF16)
        wg32 = sbt("wg32", [128, 8, 8], F32)
        onesb = sbt("onesb", [128, 1], BF16)
        onesf = sbt("onesf", [128, 1], F32)
        ones1f = sbt("ones1f", [1, 128], F32)
        ones18f = sbt("ones18f", [1, 8], F32)
        ltf = sbt("ltf", [128, 128], F32)
        io512 = sbt("io512", [128, 512], F32)
        io8 = sbt("io8", [128, 8], F32)
        identf = sbt("identf", [128, 128], F32)
        identb = sbt("identb", [128, 128], BF16)
        maskm = sbt("maskm", [128, 256], F32)
        mask0 = sbt("mask0", [128, 128], F32)
        wrapms = sbt("wrapms", [1, NCOL], F32)
        bq_s = sbt("bq_s", [128, 8], F32)
        bk_s = sbt("bk_s", [128, 8], F32)
        rowst = sbt("rowst", [1, D], F32)
        bg_s = sbt("bg_s", [8, 1], F32)
        negS = sbt("negS", [8, 1], F32)
        b1t = sbt("b1t", [128, 32, 8], F32)
        cvals = sbt("cvals", [128, 8], F32)
        idx1 = sbt("idx1", [128, NCOL], F32)
        idx2 = sbt("idx2", [128, NCOL], F32)
        dpl = sbt("dpl", [128, NCOL], F32)
        g1p = sbt("g1p", [128, NCOL], F32)
        g2p = sbt("g2p", [128, NCOL], F32)
        colf = sbt("colf", [128, 2, 32], F32)
        colov = sbt("colov", [128, 2, 32], F32)
        gsel = sbt("gsel", [128, 2, 32], F32)
        probe_h = sbt("probe_reg", [128, 32], F32)
        stack_top = nc.lookup_mloc(probe_h).addr + 128
        RB = (stack_top + 63) // 64 * 64
        RC = RB + 65536
        RD = RC + 57344
        assert RD + 65536 <= 229344, (stack_top, RB, RC, RD)
        wat = sba("wat", [128, 4, 8, D], BF16, RB)
        xnf = sba("xnf", [128, 8, S], BF16, RD)
        bvb = sba("bvb", [128, D], F32, RC + 47168)
        bob = sba("bob", [128, D], F32, RC + 51264)

        names = ["ld", "ldg", "tt", "vv", "ss", "pp", "st"]
        sml = {n: sem(n) for n in names}
        LD, LDG, TT, VV, SS, PP, ST = (sml[n] for n in names)

        def dma(eng_name, out_ap, in_ap, s=LD, waits=()):
            return p.add(eng_name, lambda e, o=out_ap, i=in_ap: e.dma_start(out=o, in_=i),
                         waits=waits, inc=(s, 16))

        def vop(fn, waits=(), s=VV):
            return p.add("vector", fn, waits=waits, inc=(s, 1))

        def sop(fn, waits=(), s=SS):
            return p.add("scalar", fn, waits=waits, inc=(s, 1))

        def gop(fn, waits=(), s=PP):
            return p.add("gpsimd", fn, waits=waits, inc=(s, 1))

        def top(fn, waits=(), s=TT):
            return p.add("tensor", fn, waits=waits, inc=(s, 1))

        def mm(out, lhsT, rhs, start, stop):
            return lambda t: t.matmul(out, lhsT, rhs, start=start, stop=stop)

        for dst, src in [(ltf, c_lt), (io512, c_io512), (io8, c_io8),
                         (identf, c_identf), (identb, c_identb), (maskm, c_mask), (mask0, c_mask0),
                         (bq_s, bqr), (bk_s, bkr)]:
            dma("sync", dst[:, :], src[:, :])
        dma("sync", wrapms[:, :], wrapm[:, :])
        dma("sync", bg_s[:, :], bgc[:, :])
        dma("sync", cvals[:, :], c_vals[:, :])
        dma("sync", b1t[:, :, :], b1r[:, :, :])

        dma("sync", wg32[:, :, :], wgi.rearrange("(c p) e -> p c e", p=128))
        ld_done = p.val(LD)

        ldg_done = p.val(LDG)

        vop(lambda v: v.memset(onesb[:], 1.0))
        vop(lambda v: v.memset(onesf[:], 1.0))
        vop(lambda v: v.memset(ones1f[:], 1.0))
        vop(lambda v: v.memset(ones18f[:], 1.0))
        vop(lambda v: v.memset(g9.ap().rearrange("p a b -> p (a b)"), 0.0))
        v_init = vop(lambda v: v.memset(g9[:, :, 8:9], 1.0), waits=[(LDG, ldg_done), (VV, p.val(VV))])
        dma("gpsimd", g9[:, :, 0:8], wgi.rearrange("(c p) e -> p c e", p=128), s=LDG,
            waits=[(VV, v_init)])
        ldg_done = p.val(LDG)

        es0 = ExitStack()
        pS = es0.enter_context(nc.psum_tensor("pS", [1, 512], F32))
        pS8 = es0.enter_context(nc.psum_tensor("pS8", [128, 512], F32))
        pbc = es0.enter_context(nc.psum_tensor("pbc", [128, 2, 512], F32))

        for ic in range(8):
            w = [(LD, ld_done), (VV, v_init)] if ic == 0 else ()
            t_s = top(mm(pS[0:1, 0:8], onesf[:, :], wg32[:, ic, :], ic == 0, ic == 7), waits=w)
        srow = sbt("srow", [1, 8], F32)
        v_sr = vop(lambda v: v.tensor_copy(srow[:, :], pS[0:1, 0:8]), waits=[(TT, t_s)])
        t_tr = top(mm(pS8[0:8, 0:8], srow[0:1, 0:8], identf[0:1, 0:8], True, True),
                   waits=[(VV, v_sr)])
        v_negs = vop(lambda v: v.tensor_scalar(negS[:, :], pS8[0:8, 0:1], -1.0, None, ALU.mult),
                     waits=[(TT, t_tr)])

        for bi, (row_src, dst) in enumerate([(bvrow, bvb), (borow, bob)]):
            lv = dma("sync", rowst[:, :], row_src[:, :],
                     waits=([(TT, p.val(TT)), (VV, p.val(VV))] if bi > 0 else ()))
            for half in range(2):
                t_b = top(mm(pbc[:, half, :], ones1f[0:1, :],
                             rowst[0:1, half * 512:(half + 1) * 512], True, True),
                          waits=[(LD, lv)])
                vop(lambda v, d=dst, h=half: v.tensor_copy(
                    d[:, h * 512:(h + 1) * 512], pbc[:, h, :]), waits=[(TT, t_b)])
        p0_tt = p.val(TT)
        p0_vv = p.val(VV)
        es0.close()

        xg = sba("xg", [128, 2, 8, 512], BF16, RB)
        sqt = sba("sqt", [128, 2, 2, 512], BF16, RB + 16384)
        lg9 = sba("lg9", [8, 2, 512], F32, RB + 24576)
        mus = sba("mus", [1, 2, 3, 512], F32, RB + 28672)
        ltt = sba("ltt", [8, 2, 512], F32, RB + 40960)
        l4 = sba("l4", [128, 2, 4, 8], F32, RB + 45056)
        scr = sba("scr", [128, 24], F32, RB + 45312)
        eq4 = sba("eq4", [128, 4, 8], F32, RB + 45440)
        tmp4 = sba("tmp4", [128, 4, 8], F32, RB + 45568)
        m14 = sba("m14", [128, 4], F32, RB + 45696)
        m24 = sba("m24", [128, 4], F32, RB + 45728)
        es1 = ExitStack()
        pL = es1.enter_context(nc.psum_tensor("pL", [8, 2, 512], F32))
        pQ = es1.enter_context(nc.psum_tensor("pQ", [1, 2, 512], F32))
        pB = es1.enter_context(nc.psum_tensor("pB", [8, 2, 512], F32))
        pT2 = es1.enter_context(nc.psum_tensor("pT2", [128, 2, 512], F32))

        NF = 64
        ldg_f = {}
        f_v3 = {}
        f_mml = {-2: p0_tt, -1: p0_tt}
        f_mmq = {-2: p0_tt, -1: p0_tt}
        f_vend = {-2: p0_vv, -1: p0_vv}
        f_trp = {-2: p0_tt, -1: p0_tt}
        for f in range(NF):
            b = f % 2
            for ic in range(8):
                w = [(TT, f_mmq[f - 2])] if ic == 0 else ()
                p.add("gpsimd", lambda e, bb=b, i=ic, ff=f: e.dma_start(
                    out=xg[:, bb, i, :], in_=xfm[i * 128:(i + 1) * 128, ff * 512:(ff + 1) * 512]),
                    waits=w, inc=(LDG, 16))
            ldg_f[f] = p.val(LDG)
            for ic in range(8):
                w = [(LDG, ldg_f[f]), (VV, f_vend[f - 2])] if ic == 0 else ()
                p.add("tensor", mm(pL[:, b, :], g9[:, ic, 0:8], xg[:, b, ic, :], ic == 0, ic == 7),
                      waits=w, inc=(TT, 1))
            for ic in range(8):
                w2_ = [(VV, f_v3[f - 1])] if (ic == 0 and f >= 1) else ()
                p.add("tensor", mm(pQ[:, 1, :], onesb[:, :], xg[:, b, ic, :], ic == 0, ic == 7),
                      waits=w2_, inc=(TT, 1))
            f_mml[f] = p.val(TT)
            qvals = []
            for ic in range(8):
                w = [(LDG, ldg_f[f])]
                if ic >= 2:
                    w.append((TT, qvals[ic - 2]))
                if ic == 0 and f >= 2:
                    w.append((TT, f_mmq[f - 2]))
                if ic == 0 and f >= 1:
                    w.append((VV, f_v3[f - 1]))
                s_q = sop(lambda s, bb=b, i=ic: s.activation(sqt[:, bb, i % 2, :], xg[:, bb, i, :], AF.Square),
                          waits=w)
                qvals.append(p.add("tensor", mm(pQ[:, 0, :], onesb[:, :], sqt[:, b, ic % 2, :], ic == 0, ic == 7),
                                   waits=[(SS, s_q)], inc=(TT, 1)))
            f_mmq[f] = qvals[-1]
            v1 = vop(lambda v, bb=b: v.tensor_copy(lg9[:, bb, :], pL[:, bb, :]), waits=[(TT, f_mml[f])])
            v2 = vop(lambda v, bb=b: v.tensor_scalar(mus[:, bb, 0, :], pQ[0:1, 1, :], 1.0 / 1024.0, None, ALU.mult))
            v3 = vop(lambda v, bb=b: v.tensor_scalar(mus[:, bb, 1, :], pQ[0:1, 0, :], 1.0 / 1024.0, None, ALU.mult),
                     waits=[(TT, f_mmq[f])])
            f_v3[f] = v3
            v4 = vop(lambda v, bb=b: v.tensor_tensor(mus[:, bb, 2, :], mus[:, bb, 0, :], mus[:, bb, 0, :], ALU.mult))
            v5 = vop(lambda v, bb=b: v.tensor_tensor(mus[:, bb, 1, :], mus[:, bb, 1, :], mus[:, bb, 2, :], ALU.subtract))
            s1 = sop(lambda s, bb=b: s.activation(mus[:, bb, 2, :], mus[:, bb, 1, :], AF.Ln, bias=cvals[0:1, 0:1], scale=1.0),
                     waits=[(VV, v5)])
            s2 = sop(lambda s, bb=b: s.activation(mus[:, bb, 2, :], mus[:, bb, 2, :], AF.Exp, bias=0.0, scale=cvals[0:1, 1:2]))
            t_mb = top(mm(pB[:, b, :], ones18f[0:1, 0:8], mus[0:1, b, 0, :], True, True),
                       waits=[(SS, s2)])
            v6 = vop(lambda v, bb=b: v.scalar_tensor_tensor(ltt[:, bb, :], pB[:, bb, :], negS[0:8, 0:1],
                                                            lg9[0:8, bb, :], ALU.mult, ALU.add),
                     waits=[(TT, t_mb)])
            t_rb = top(mm(pB[:, b, :], ones18f[0:1, 0:8], mus[0:1, b, 2, :], True, True),
                       waits=[(VV, v6)])
            v7 = vop(lambda v, bb=b: v.tensor_tensor(ltt[:, bb, :], ltt[:, bb, :], pB[:, bb, :], ALU.mult),
                     waits=[(TT, t_rb)])
            v8 = vop(lambda v, bb=b: v.tensor_scalar(ltt[:, bb, :], ltt[:, bb, :], bg_s[0:8, 0:1], None, ALU.add))
            for j in range(4):
                w = [(VV, v8)]
                if j == 0 and f >= 2:
                    w.append((VV, f_vend[f - 2]))
                p.add("tensor", lambda t, bb=b, jj=j: t.transpose(
                    pT2[:, bb, jj * 8:(jj + 1) * 8], ltt[0:8, bb, jj * 128:(jj + 1) * 128], identf[0:8, 0:8]),
                    waits=(w if j == 0 else ()), inc=((TT, 1) if j == 3 else None))
            f_trp[f] = p.val(TT)
            v9 = vop(lambda v, bb=b: v.tensor_copy(l4[:, bb, :, :],
                                                   pT2[:, bb, 0:32].rearrange("p (a b) -> p a b", a=4)),
                     waits=[(TT, f_trp[f])])
            for j in range(4):
                vop(lambda v, bb=b, jj=j: v.reduce_max(m14[:, jj:jj + 1], l4[:, bb, jj, :], axis=AX.X))
            for j in range(4):
                vop(lambda v, bb=b, jj=j: v.tensor_scalar(eq4[:, jj, :], l4[:, bb, jj, :],
                                                          m14[:, jj:jj + 1], None, ALU.is_equal))
            for j in range(4):
                vop(lambda v, jj=j: v.tensor_tensor(tmp4[:, jj, :], eq4[:, jj, :], io8[:, :], ALU.mult))
            for j in range(4):
                vop(lambda v, jj=j, c=4 * f + j: v.reduce_sum(idx1[:, c:c + 1], tmp4[:, jj, :], axis=AX.X))
            for j in range(4):
                vop(lambda v, bb=b, jj=j: v.scalar_tensor_tensor(tmp4[:, jj, :], eq4[:, jj, :], -BIG,
                                                                 l4[:, bb, jj, :], ALU.mult, ALU.add))
            for j in range(4):
                vop(lambda v, jj=j: v.reduce_max(m24[:, jj:jj + 1], tmp4[:, jj, :], axis=AX.X))
            for j in range(4):
                vop(lambda v, jj=j: v.tensor_scalar(eq4[:, jj, :], tmp4[:, jj, :],
                                                    m24[:, jj:jj + 1], None, ALU.is_equal))
            for j in range(4):
                vop(lambda v, jj=j: v.tensor_tensor(tmp4[:, jj, :], eq4[:, jj, :], io8[:, :], ALU.mult))
            for j in range(4):
                vop(lambda v, jj=j, c=4 * f + j: v.reduce_sum(idx2[:, c:c + 1], tmp4[:, jj, :], axis=AX.X))
            for j in range(4):
                vop(lambda v, jj=j, c=4 * f + j: v.tensor_tensor(dpl[:, c:c + 1], m24[:, jj:jj + 1],
                                                                 m14[:, jj:jj + 1], ALU.subtract))
            f_vend[f] = p.val(VV)
        s_g = sop(lambda s: s.activation(g2p[:, :], dpl[:, :], AF.Exp, bias=0.0, scale=1.0),
                  waits=[(VV, p.val(VV))])
        vop(lambda v: v.tensor_scalar(g2p[:, :], g2p[:, :], 1.0, None, ALU.add), waits=[(SS, s_g)])
        vop(lambda v: v.tensor_copy(scr[:, 0:1], onesf[:, :]))
        vop(lambda v: v.reciprocal(g1p[:, :], g2p[:, :]))
        vop(lambda v: v.tensor_copy(scr[:, 0:1], onesf[:, :]))
        vop(lambda v: v.tensor_scalar(g2p[:, :], g1p[:, :], -1.0, 1.0, ALU.mult, ALU.add))
        gates_done = p.val(VV)
        es1.close()
        p1_tt = p.val(TT)

        ind = sba("ind", [128, 2, 8, NCOL], F32, RD)
        exc = sba("exc", [128, 2, 8, NCOL], F32, RD + 16384)
        tot = sba("tot", [1, 2, 8, NCOL], F32, RD + 32768)
        cof = sba("cof", [1, 2, 8, NCOL], F32, RD + 49152)
        rke = sba("rke", [128, 2, 8, 32], F32, RC)
        ove = sba("ove", [128, 2, 8, 32], F32, RC + 8192)
        kco = sba("kco", [128, 2, 8, 32], F32, RC + 16384)
        ids = sba("ids", [128, 2, 8, 32], F32, RC + 24576)
        wrow = sba("wrow", [1, 2 * 8 * 4], F32, RC + 32768)
        ovsc = sba("ovsc", [1, 2, 8, 32], F32, RC + 33024)
        ovco = sba("ovco", [1, 2, 8, 32], F32, RC + 36096)
        cn0 = sba("cn0", [1, 8, 32], F32, RC + 39168)

        es2 = ExitStack()
        pE = es2.enter_context(nc.psum_tensor("pE", [128, 2, 512], F32))
        pCt = es2.enter_context(nc.psum_tensor("pCt", [1, 2, 512], F32))
        pBC = es2.enter_context(nc.psum_tensor("pBC", [128, 2, 512], F32))

        pv0 = p.val(VV)
        for k in range(2):
            for e in range(E):
                src_pl = idx1 if k == 0 else idx2
                vop(lambda v, kk=k, ee=e, sp=src_pl: v.tensor_scalar(
                    ind[:, kk, ee, :], sp[:, :], float(ee), None, ALU.is_equal),
                    waits=([(VV, gates_done)] if (k == 0 and e == 0) else ()))
        ind_done = p.val(VV)
        mme = {}
        for k in range(2):
            for e in range(E):
                i = k * 8 + e
                w = [(VV, ind_done)] if i == 0 else []
                if i >= 2:
                    w.append((VV, mme[i - 2]))
                p.add("tensor", mm(pE[:, i % 2, 0:256], ltf[:, :], ind[:, k, e, :], True, True),
                      waits=w, inc=(TT, 1))
                p.add("tensor", mm(pCt[:, i % 2, 0:256], onesf[:, :], ind[:, k, e, :], True, True),
                      inc=(TT, 1))
                tvv = p.val(TT)
                vop(lambda v, kk=k, ee=e, bb=i % 2: v.tensor_copy(exc[:, kk, ee, :], pE[:, bb, 0:256]),
                    waits=[(TT, tvv)])
                mme[i] = vop(lambda v, kk=k, ee=e, bb=i % 2: v.tensor_copy(tot[:, kk, ee, :], pCt[:, bb, 0:256]))
        v_w1 = vop(lambda v: v.tensor_tensor(cof.rearrange("a b c d -> a (b c) d"),
                                             tot.rearrange("a b c d -> a (b c) d"),
                                             wrapms[0:1, :].rearrange("a (b d) -> a b d", b=1).broadcast(1, 16) if False else tot.rearrange("a b c d -> a (b c) d"), ALU.mult))
        p.ops["vector"].pop()
        p.cnt[VV] -= 1
        for k in range(2):
            for e in range(E):
                vop(lambda v, kk=k, ee=e: v.tensor_tensor(cof[:, kk, ee, :], tot[:, kk, ee, :],
                                                          wrapms[:, :], ALU.mult))
        v_w3 = vop(lambda v: v.reduce_sum(wrow[0:1, 16:24].rearrange("a (b c) -> a b c", b=8),
                                          tot[:, 0, :, :], axis=AX.X))
        v_w2 = vop(lambda v: v.reduce_sum(wrow[0:1, 0:16].rearrange("a (b c) -> a b c", b=16),
                                          cof.rearrange("a b c d -> a (b c) d"), axis=AX.X))
        v_cs = []
        for k in range(2):
            for e in range(E):
                vop(lambda v, kk=k, ee=e: v.memset(cof[:, kk, ee, 0:1], 0.0))
        v_w4 = vop(lambda v: v.tensor_tensor(wrow[0:1, 8:16], wrow[0:1, 8:16], wrow[0:1, 16:24], ALU.add))
        for k in range(2):
            for e in range(E):
                vop(lambda v, kk=k, ee=e: v.tensor_tensor_scan(
                    cof[:, kk, ee, 1:NCOL], tot[:, kk, ee, 0:NCOL - 1], tot[:, kk, ee, 0:NCOL - 1],
                    0.0, ALU.add, ALU.bypass))
        for k in range(2):
            for e in range(E):
                i = k * 8 + e
                vop(lambda v, kk=k, ee=e, ii=i: v.tensor_scalar(
                    tot[:, kk, ee, :], cof[:, kk, ee, :], wrow[0:1, ii:ii + 1], None, ALU.add))
        for k in range(2):
            for e in range(E):
                vop(lambda v, kk=k, ee=e: v.tensor_scalar(
                    tot[:, kk, ee, :], tot[:, kk, ee, :], -1.0, float(CAP), ALU.mult, ALU.add))
        thr_done = p.val(VV)
        if debug >= 1:
            o_w = dbgten("g_wrow", [1, 32])
            p.add("sync", lambda e, oo=o_w: e.dma_start(out=oo[:, :], in_=wrow[0:1, 0:32]),
                  waits=[(VV, thr_done)], inc=(ST, 16))
            o_t = dbgten("g_thr", [1, 16 * NCOL])
            p.add("sync", lambda e, oo=o_t: e.dma_start(
                out=oo[:, :], in_=tot.rearrange("a b c d -> a (b c d)")),
                waits=[(VV, thr_done)], inc=(ST, 16))
            o_e = dbgten("g_exc", [128, 16 * NCOL])
            p.add("sync", lambda e, oo=o_e: e.dma_start(
                out=oo[:, :], in_=exc.rearrange("p b c d -> p (b c d)")),
                waits=[(VV, thr_done)], inc=(ST, 16))
        for k in range(2):
            for e in range(E):
                i = k * 8 + e
                w = ([(VV, thr_done)] + ([(ST, p.val(ST))] if debug >= 1 else [])) if i == 0 else []
                if i >= 2:
                    w.append((VV, kdone[i - 2]))
                t_bb = p.add("tensor", mm(pBC[:, i % 2, 0:256], ones1f[0:1, :], tot[0:1, k, e, :], True, True),
                             waits=w, inc=(TT, 1))
                vop(lambda v, kk=k, ee=e, bb=i % 2: v.tensor_tensor(
                    exc[:, kk, ee, :], exc[:, kk, ee, :], pBC[:, bb, 0:256], ALU.is_lt),
                    waits=[(TT, t_bb)])
                if i == 0:
                    kdone = {}
                kdone[i] = vop(lambda v, kk=k, ee=e: v.tensor_tensor(
                    ind[:, kk, ee, :], ind[:, kk, ee, :], exc[:, kk, ee, :], ALU.mult))
        keep_done = p.val(VV)
        rdone = {}
        for k in range(2):
            for e in range(E):
                i = k * 8 + e
                w = [(VV, keep_done)] if i == 0 else []
                if i >= 2:
                    w.append((VV, rdone[i - 2]))
                p.add("tensor", mm(pE[:, i % 2, 0:32], ltf[:, :], ind[:, k, e, 0:32], True, True),
                      waits=w, inc=(TT, 1))
                p.add("tensor", mm(pCt[:, i % 2, 0:32], onesf[:, :], ind[:, k, e, 0:32], True, True),
                      inc=(TT, 1))
                tvv = p.val(TT)
                vop(lambda v, kk=k, ee=e, bb=i % 2: v.tensor_copy(rke[:, kk, ee, :], pE[:, bb, 0:32]),
                    waits=[(TT, tvv)])
                vop(lambda v, kk=k, ee=e: v.tensor_copy(ids[:, kk, ee, :], ind[:, kk, ee, 0:32]))
                if k == 0:
                    rdone[i] = vop(lambda v, ee=e, bb=i % 2: v.tensor_copy(cn0[:, ee, :], pCt[:, bb, 0:32]))
                else:
                    rdone[i] = vop(lambda v: v.tensor_copy(wrow[0:1, 24:25], onesf[0:1, 0:1]))
        r1done = p.val(VV)
        c1done = {}
        for e in range(E):
            w = [(VV, r1done)] if e == 0 else []
            if e >= 2:
                w.append((VV, c1done[e - 2]))
            t_bb = p.add("tensor", mm(pBC[:, e % 2, 0:32], ones1f[0:1, :], cn0[0:1, e, :], True, True),
                         waits=w, inc=(TT, 1))
            c1done[e] = vop(lambda v, ee=e, bb=e % 2: v.tensor_tensor(
                rke[:, 1, ee, :], rke[:, 1, ee, :], pBC[:, bb, 0:32], ALU.add),
                waits=[(TT, p.val(TT))])
        rank_done = p.val(VV)
        for k in range(2):
            for e in range(E):
                vop(lambda v, kk=k, ee=e: v.tensor_scalar(
                    exc[:, kk, ee, 0:32], rke[:, kk, ee, :], float(W), None, ALU.is_ge),
                    waits=([(VV, rank_done)] if (k == 0 and e == 0) else ()))
        for k in range(2):
            for e in range(E):
                vop(lambda v, kk=k, ee=e: v.tensor_tensor(
                    exc[:, kk, ee, 0:32], exc[:, kk, ee, 0:32], ind[:, kk, ee, 0:32], ALU.mult))
        ov_done = p.val(VV)
        odone = {}
        for k in range(2):
            for e in range(E):
                i = k * 8 + e
                w = [(VV, ov_done)] if i == 0 else []
                if i >= 2:
                    w.append((VV, odone[i - 2]))
                p.add("tensor", mm(pE[:, i % 2, 0:32], ltf[:, :], exc[:, k, e, 0:32], True, True),
                      waits=w, inc=(TT, 1))
                p.add("tensor", mm(pCt[:, i % 2, 0:32], onesf[:, :], exc[:, k, e, 0:32], True, True),
                      inc=(TT, 1))
                tvv = p.val(TT)
                vop(lambda v, kk=k, ee=e, bb=i % 2: v.tensor_copy(ove[:, kk, ee, :], pE[:, bb, 0:32]),
                    waits=[(TT, tvv)])
                odone[i] = vop(lambda v, kk=k, ee=e, bb=i % 2: v.tensor_copy(ovsc[:, kk, ee, :], pCt[:, bb, 0:32]))
        ovt_done = p.val(VV)
        for e in range(E):
            vop(lambda v, ee=e: v.memset(ovco[:, 0, ee, 0:1], 0.0),
                waits=([(VV, ovt_done)] if e == 0 else ()))
        for e in range(E):
            vop(lambda v, ee=e: v.tensor_tensor_scan(
                ovco[:, 0, ee, 1:32], ovsc[:, 0, ee, 0:31], ovsc[:, 0, ee, 0:31], 0.0, ALU.add, ALU.bypass))
        for e in range(E):
            vop(lambda v, ee=e: v.tensor_tensor(wrow[0:1, 32 + ee:33 + ee], ovco[0:1, 0, ee, 31:32],
                                                ovsc[0:1, 0, ee, 31:32], ALU.add))
        for e in range(E):
            vop(lambda v, ee=e: v.tensor_copy(ovco[:, 1, ee, 0:1], wrow[0:1, 32 + ee:33 + ee]))
        for e in range(E):
            vop(lambda v, ee=e: v.tensor_tensor_scan(
                ovco[:, 1, ee, 1:32], ovsc[:, 1, ee, 0:31], ovsc[:, 1, ee, 0:31], 0.0, ALU.add, ALU.bypass))
        for e in range(E):
            vop(lambda v, ee=e: v.tensor_scalar(
                ovco[:, 1, ee, 1:32], ovco[:, 1, ee, 1:32], wrow[0:1, 32 + ee:33 + ee], None, ALU.add))
        ovco_done = p.val(VV)
        obdone = {}
        for k in range(2):
            for e in range(E):
                i = k * 8 + e
                w = [(VV, ovco_done)] if i == 0 else []
                if i >= 2:
                    w.append((VV, obdone[i - 2]))
                t_bb = p.add("tensor", mm(pBC[:, i % 2, 0:32], ones1f[0:1, :], ovco[0:1, k, e, :], True, True),
                             waits=w, inc=(TT, 1))
                obdone[i] = vop(lambda v, kk=k, ee=e, bb=i % 2: v.tensor_tensor(
                    ove[:, kk, ee, :], ove[:, kk, ee, :], pBC[:, bb, 0:32], ALU.add),
                    waits=[(TT, p.val(TT))])
        ovr_done = p.val(VV)
        prods = sba("prods", [128, 2, 32, 8], F32, RD + 49152)
        for k in range(2):
            for e in range(E):
                vop(lambda v, kk=k, ee=e: v.tensor_tensor(
                    prods[:, kk, :, ee], ids[:, kk, ee, :], rke[:, kk, ee, :], ALU.mult),
                    waits=([(VV, ovr_done)] if (k == 0 and e == 0) else ()))
        for k in range(2):
            vop(lambda v, kk=k: v.reduce_sum(colf[:, kk, :], prods[:, kk, :, :], axis=AX.X))
        for k in range(2):
            for e in range(E):
                vop(lambda v, kk=k, ee=e: v.tensor_tensor(
                    prods[:, kk, :, ee], ids[:, kk, ee, :], ove[:, kk, ee, :], ALU.mult))
        for k in range(2):
            vop(lambda v, kk=k: v.reduce_sum(colov[:, kk, :], prods[:, kk, :, :], axis=AX.X))
        for k in range(2):
            for e in range(E):
                vop(lambda v, kk=k, ee=e: v.tensor_copy(prods[:, kk, :, ee], ids[:, kk, ee, :]))
        for k in range(2):
            vop(lambda v, kk=k: v.reduce_sum(kco[:, kk, 0, :], prods[:, kk, :, :], axis=AX.X))
        sel_done = p.val(VV)
        for k in range(2):
            vop(lambda v, kk=k: v.tensor_scalar(kco[:, kk, 2, :], colf[:, kk, :], float(W), None, ALU.is_lt),
                waits=([(VV, sel_done)] if k == 0 else ()))
        for k in range(2):
            vop(lambda v, kk=k: v.tensor_tensor(kco[:, kk, 2, :], kco[:, kk, 2, :], kco[:, kk, 0, :], ALU.mult))
        for k in range(2):
            vop(lambda v, kk=k: v.tensor_scalar(kco[:, kk, 3, :], colov[:, kk, :], float(OV), None, ALU.is_lt))
        for k in range(2):
            vop(lambda v, kk=k: v.tensor_tensor(kco[:, kk, 3, :], kco[:, kk, 3, :], kco[:, kk, 0, :], ALU.mult))
        for k in range(2):
            vop(lambda v, kk=k: v.tensor_scalar(kco[:, kk, 4, :], colf[:, kk, :], float(W), None, ALU.is_ge))
        for k in range(2):
            vop(lambda v, kk=k: v.tensor_tensor(kco[:, kk, 3, :], kco[:, kk, 3, :], kco[:, kk, 4, :], ALU.mult))
        for k in range(2):
            ip = idx1 if k == 0 else idx2
            vop(lambda v, kk=k, ipl=ip: v.scalar_tensor_tensor(
                kco[:, kk, 6, :], ipl[:, 0:32], float(W), colf[:, kk, :], ALU.mult, ALU.add))
        for k in range(2):
            vop(lambda v, kk=k: v.tensor_scalar(kco[:, kk, 5, :], kco[:, kk, 2, :], -BIG, BIG, ALU.mult, ALU.add))
        for k in range(2):
            ip = idx1 if k == 0 else idx2
            vop(lambda v, kk=k, ipl=ip: v.scalar_tensor_tensor(
                kco[:, kk, 1, :], ipl[:, 0:32], float(OV), colov[:, kk, :], ALU.mult, ALU.add))
        for k in range(2):
            vop(lambda v, kk=k: v.tensor_tensor(colf[:, kk, :], kco[:, kk, 6, :], kco[:, kk, 5, :], ALU.add))
        for k in range(2):
            vop(lambda v, kk=k: v.tensor_scalar(kco[:, kk, 7, :], kco[:, kk, 3, :], -BIG, BIG, ALU.mult, ALU.add))
        for k in range(2):
            vop(lambda v, kk=k: v.tensor_tensor(colov[:, kk, :], kco[:, kk, 1, :], kco[:, kk, 7, :], ALU.add))
        for k in range(2):
            gp = g1p if k == 0 else g2p
            vop(lambda v, kk=k, g=gp: v.tensor_copy(gsel[:, kk, :], g[:, 0:32]))
        p2_done = p.val(VV)
        es2.close()
        p2_tt = p.val(TT)
        if debug >= 1:
            for nm, srcp in [("g_colf", colf), ("g_colov", colov), ("g_gsel", gsel)]:
                o = dbgten(nm, [128, 64])
                p.add("sync", lambda e, oo=o, ss_=srcp: e.dma_start(
                    out=oo[:, :], in_=ss_.rearrange("p a b -> p (a b)")),
                    waits=[(VV, p2_done)], inc=(ST, 16))
        if debug >= 2:
            o = dbgten("g_kept", [128, 16 * NCOL])
            p.add("sync", lambda e, oo=o: e.dma_start(
                out=oo[:, :], in_=ind.rearrange("p a b c -> p (a b c)")),
                waits=[(VV, p2_done)], inc=(ST, 16))
            o2 = dbgten("g_rke", [128, 2 * 8 * 32])
            p.add("sync", lambda e, oo=o2: e.dma_start(
                out=oo[:, :], in_=rke.rearrange("p a b c -> p (a b c)")),
                waits=[(VV, p2_done)], inc=(ST, 16))
            o4 = dbgten("g_kco", [128, 2 * 8 * 32])
            p.add("sync", lambda e, oo=o4: e.dma_start(
                out=oo[:, :], in_=kco.rearrange("p a b c -> p (a b c)")),
                waits=[(VV, p2_done)], inc=(ST, 16))
            o3 = dbgten("g_ove", [128, 2 * 8 * 32])
            p.add("sync", lambda e, oo=o3: e.dma_start(
                out=oo[:, :], in_=ove.rearrange("p a b c -> p (a b c)")),
                waits=[(VV, p2_done)], inc=(ST, 16))
        watw = [(VV, gates_done), (TT, p1_tt), (SS, p.val(SS))]
        for wi, wsrc in enumerate([wq, wk, wv, wo]):
            for ic in range(8):
                dma("gpsimd", wat[:, wi, ic, :], wsrc[ic * 128:(ic + 1) * 128, :], s=LDG,
                    waits=(watw if (wi == 0 and ic == 0) else ()))
        wat_done = p.val(LDG)

        if debug >= 2:
            for nm, src, shp in [("g_lg9", lg9, [8, 2 * 512]), ("g_mus", mus, [1, 2 * 3 * 512]),
                                 ("g_ltt", ltt, [8, 2 * 512]), ("g_l4", l4, [128, 2 * 4 * 8])]:
                o = dbgten(nm, shp)
                p.add("sync", lambda e, oo=o, ss_=src: e.dma_start(
                    out=oo[:, :], in_=ss_.rearrange("p a b -> p (a b)") if len(ss_.shape) == 3 else ss_.rearrange("p a b c -> p (a b c)")),
                    waits=[(VV, gates_done)], inc=(ST, 16))
        if debug >= 1:
            for nm, src in [("g_idx1", idx1), ("g_idx2", idx2), ("g_g1", g1p), ("g_g2", g2p),
                            ("g_dpl", dpl)]:
                o = dbgten(nm, [128, NCOL])
                p.add("sync", lambda e, oo=o, ss_=src: e.dma_start(out=oo[:, :], in_=ss_[:, :]),
                      waits=[(VV, gates_done)], inc=(ST, 16))

        p.add("sync", lambda e: e.nop(),
              waits=[(ST, p.val(ST)), (VV, p.val(VV)), (SS, p.val(SS)), (TT, p.val(TT)),
                     (LDG, p.val(LDG)), (LD, p.val(LD)), (PP, p.val(PP))])


        xld = sba("xld", [128, 2, D], F32, RC)
        sqx = sba("sqx", [128, 2, D], F32, RC + 8192)
        st3 = sba("st3", [128, 2, 8], F32, RC + 16384)
        xnt = sba("xnt", [128, 2, D], BF16, RC + 16896)
        es3 = ExitStack()
        pTR = es3.enter_context(nc.psum_tensor("pTR", [128, 2, 1024], BF16))
        t3_ld = {}
        t3_sq = {}
        t3_v1 = {}
        t3_xn = {}
        t3_tp = {}
        t3_ev = {-2: 0, -1: 0}
        t3_st = {}
        for Ti in range(NTILE):
            b = Ti % 2
            w = [(VV, p2_done)] if Ti == 0 else []
            if Ti >= 2:
                w.append((VV, t3_xn[Ti - 2]))
            t3_ld[Ti] = dma("sync", xld[:, b, :], x_sh[Ti * 128:(Ti + 1) * 128, :], waits=w)
            v_s = vop(lambda v, bb=b: v.reduce_sum(st3[:, bb, 0:1], xld[:, bb, :], axis=AX.X),
                      waits=[(LD, t3_ld[Ti])])
            s_sq = sop(lambda s, bb=b: s.activation(sqx[:, bb, :], xld[:, bb, :], AF.Square),
                       waits=[(LD, t3_ld[Ti])] + ([(VV, t3_v1[Ti - 2])] if Ti >= 2 else []))
            v_s2 = vop(lambda v, bb=b: v.reduce_sum(st3[:, bb, 1:2], sqx[:, bb, :], axis=AX.X),
                       waits=[(SS, s_sq)])
            v_mu = vop(lambda v, bb=b: v.tensor_scalar(st3[:, bb, 2:3], st3[:, bb, 0:1], 1.0 / 1024.0, None, ALU.mult))
            v_e2 = vop(lambda v, bb=b: v.tensor_scalar(st3[:, bb, 3:4], st3[:, bb, 1:2], 1.0 / 1024.0, None, ALU.mult))
            v_m2 = vop(lambda v, bb=b: v.tensor_tensor(st3[:, bb, 4:5], st3[:, bb, 2:3], st3[:, bb, 2:3], ALU.mult))
            vop(lambda v, bb=b: v.tensor_copy(sqx[:, bb, 0:1], onesf[:, :]))
            v_va = vop(lambda v, bb=b: v.tensor_tensor(st3[:, bb, 5:6], st3[:, bb, 3:4], st3[:, bb, 4:5], ALU.subtract))
            t3_v1[Ti] = v_va
            s_ln = sop(lambda s, bb=b: s.activation(st3[:, bb, 6:7], st3[:, bb, 5:6], AF.Ln,
                                                    bias=cvals[:, 0:1], scale=1.0), waits=[(VV, v_va)])
            s_rs = sop(lambda s, bb=b: s.activation(st3[:, bb, 7:8], st3[:, bb, 6:7], AF.Exp,
                                                    bias=0.0, scale=cvals[:, 1:2]))
            t3_xn[Ti] = vop(lambda v, bb=b: v.tensor_scalar(xnt[:, bb, :], xld[:, bb, :],
                                                            st3[:, bb, 2:3], st3[:, bb, 7:8],
                                                            ALU.subtract, ALU.mult),
                            waits=[(SS, s_rs)])
            dma("sync", d_xn[Ti * 128:(Ti + 1) * 128, :], xnt[:, b, :], waits=[(VV, t3_xn[Ti])], s=ST)
            for ic in range(8):
                ww = [(VV, t3_xn[Ti])] if ic == 0 else []
                if ic == 0 and Ti >= 2:
                    ww.append((VV, t3_ev[Ti - 2]))
                p.add("tensor", lambda t, bb=b, i=ic: t.transpose(
                    pTR[:, bb, i * 128:(i + 1) * 128], xnt[:, bb, i * 128:(i + 1) * 128], identb[:, :]),
                    waits=ww, inc=((TT, 1) if ic == 7 else None))
            t3_tp[Ti] = p.val(TT)
            for ic in range(8):
                vop(lambda v, bb=b, i=ic, tt_=Ti: v.tensor_copy(
                    xnf[:, i, tt_ * 128:(tt_ + 1) * 128], pTR[:, bb, i * 128:(i + 1) * 128]),
                    waits=([(TT, t3_tp[Ti])] if ic == 0 else ()))
            t3_ev[Ti] = p.val(VV)
        p3_done = p.val(VV)
        p3_tt = p.val(TT)
        es3.close()

        qf = sba("qf", [128, 8, 512], BF16, RC)
        kf = sba("kf", [128, 8, 640], BF16, RC + 8192)
        vt4 = sba("vt4", [128, 5, D], BF16, RC + 18432)
        apre = sba("apre", [128, 8, 512], BF16, RC + 28672)
        amsk = sba("amsk", [128, 2, 256], F32, RC + 36864)
        aexp = sba("aexp", [128, 2, 256], F32, RC + 38912)
        aprb = sba("aprb", [128, 2, 2, 256], BF16, RC + 40960)
        apT = sba("apT", [128, 2, 256], BF16, RC + 55360)
        asml = sba("asml", [128, 2, 8], F32, RC + 43008)
        aout1 = sba("aout1", [128, 2, 512], F32, RC + 43072)
        es4 = ExitStack()
        pQK = es4.enter_context(nc.psum_tensor("pQK", [128, 2, 512], F32))
        pSC = es4.enter_context(nc.psum_tensor("pSC", [128, 2, 512], F32))
        pPT = es4.enter_context(nc.psum_tensor("pPT", [128, 2, 1024], BF16))
        pAV = es4.enter_context(nc.psum_tensor("pAV", [128, 2, 512], F32))
        NMC = 8
        P4PART = 3
        MAXPH = 9
        P7PART = 4
        pSC_free = {0: 0, 1: 0}
        pPT_free = {0: 0, 1: 0}
        ost_free = {}
        mc_ttdone = {-1: p.val(TT)}
        for mc in range(NMC):
            t0 = mc * 512
            kbase = t0 - 128
            ev_q = {}
            for oc in range(8):
                bb = oc % 2
                for icc in range(8):
                    w = []
                    if oc == 0 and icc == 0:
                        w = [(VV, p3_done), (LDG, wat_done), (TT, mc_ttdone[mc - 1])]
                        if mc > 0:
                            w.append((SS, mc_ssdone))
                            w.append((VV, mc_vvdone))
                    if icc == 0 and oc >= 2:
                        w.append((SS, ev_q[oc - 2]))
                    p.add("tensor", mm(pQK[:, bb, :], wat[:, 0, icc, oc * 128:(oc + 1) * 128],
                                       xnf[:, icc, t0:t0 + 512], icc == 0, icc == 7),
                          waits=w, inc=((TT, 1) if icc == 7 else None))
                ev_q[oc] = sop(lambda s, b2=bb, o2=oc: s.activation(
                    qf[:, o2, :], pQK[:, b2, :], AF.Identity, bias=bq_s[:, o2:o2 + 1], scale=1.0),
                    waits=[(TT, p.val(TT))])
            ev_k = {}
            pieces = [(0, 320), (320, 320)] if mc > 0 else [(128, 256), (384, 256)]
            for oc in range(8):
                bb = oc % 2
                for pi, (cst, cw) in enumerate(pieces):
                    for icc in range(8):
                        w = []
                        if icc == 0 and pi == 0 and oc == 0:
                            w.append((SS, ev_q[7]))
                        if icc == 0 and pi == 0 and oc >= 2:
                            w.append((SS, ev_k[oc - 2]))
                        if icc == 0 and pi == 1:
                            w.append((SS, ev_k[oc]))
                        p.add("tensor", mm(pQK[:, bb, 0:cw],
                                           wat[:, 1, icc, oc * 128:(oc + 1) * 128],
                                           xnf[:, icc, kbase + cst:kbase + cst + cw], icc == 0, icc == 7),
                              waits=w, inc=((TT, 1) if icc == 7 else None))
                    ev_k[oc] = sop(lambda s, b2=bb, o2=oc, c2=cst, w2=cw: s.activation(
                        kf[:, o2, c2:c2 + w2], pQK[:, b2, 0:w2], AF.Identity, bias=bk_s[:, o2:o2 + 1], scale=1.0),
                        waits=[(TT, p.val(TT))])
            ev_v = {}
            vts = [0, 1, 2, 3, 4] if mc > 0 else [1, 2, 3, 4]
            for vt_i in vts:
                bb = vt_i % 2
                tok = kbase + vt_i * 128
                for half in range(2):
                    for icc in range(8):
                        w = []
                        if icc == 0 and half == 0 and vt_i == vts[0]:
                            w.append((SS, ev_k[7]))
                        if icc == 0 and half == 0 and ev_v.get(vt_i - 2):
                            w.append((VV, ev_v[vt_i - 2]))
                        if icc == 0 and half == 1:
                            w.append((VV, p.val(VV)))
                        p.add("tensor", mm(pQK[:, bb, :], xnf[:, icc, tok:tok + 128],
                                           wat[:, 2, icc, half * 512:(half + 1) * 512], icc == 0, icc == 7),
                              waits=w, inc=((TT, 1) if icc == 7 else None))
                    vop(lambda v, b2=bb, v2=vt_i, h2=half: v.tensor_tensor(
                        vt4[:, v2, h2 * 512:(h2 + 1) * 512], pQK[:, b2, :],
                        bvb[:, h2 * 512:(h2 + 1) * 512], ALU.add),
                        waits=[(TT, p.val(TT))])
                ev_v[vt_i] = p.val(VV)
            proj_done_v = p.val(VV)
            proj_done_s = p.val(SS)
            av_ev = {}
            prev_exp = {}
            prev_prb = {}
            tp_par = {}
            pend = None

            def emit_tpav(pe):
                w4p, gwp, kwidp, hpp, prbp, pbp = pe
                nkc = kwidp // 128
                tp_t = {}
                for hh in range(2):
                    for kc in range(nkc):
                        p.add("tensor", lambda t, b2=hh, k2=kc, pb2=pbp: t.transpose(
                            pPT[:, b2, k2 * 128:(k2 + 1) * 128],
                            aprb[:, pb2, b2, k2 * 128:(k2 + 1) * 128], identb[:, :]),
                            waits=([(SS, prbp[hh]), (VV, pPT_free[hh])] if kc == 0 else ()),
                            inc=((TT, 1) if kc == nkc - 1 else None))
                    tp_t[hh] = p.val(TT)
                tp_par[pbp] = tp_t[1]
                for hh in range(2):
                    vop(lambda v, b2=hh, kw=kwidp: v.tensor_copy(
                        apT[:, b2, 0:kw], pPT[:, b2, 0:kw]), waits=[(TT, tp_t[hh])])
                evT = p.val(VV)
                pPT_free[0] = pPT_free[1] = evT
                for hh in range(2):
                    h = hpp * 2 + hh
                    for kc in range(nkc):
                        vt_idx = (w4p + kc) if gwp > 0 else 1
                        w = [(VV, evT)] if (hh == 0 and kc == 0) else []
                        if hh == 0 and kc == 0 and av_ev.get((w4p, hpp - 1)):
                            w.append((VV, av_ev[(w4p, hpp - 1)]))
                        if hh == 0 and kc == 0 and w4p > 0 and av_ev.get((w4p - 1, hpp)):
                            w.append((VV, av_ev[(w4p - 1, hpp)]))
                        p.add("tensor", mm(
                            pAV[64 * (h % 2):64 * (h % 2) + 64, hpp % 2, 0:128],
                            vt4[:, vt_idx, 64 * h:64 * h + 64],
                            apT[:, hh, kc * 128:(kc + 1) * 128],
                            kc == 0, kc == nkc - 1), waits=w,
                            inc=((TT, 1) if (hh == 1 and kc == nkc - 1) else None))
                av_tt = p.val(TT)
                av_ev[(w4p, hpp)] = vop(lambda v, b2=hpp % 2, o2=hpp, w2=w4p: v.tensor_copy(
                    apre[:, o2, w2 * 128:(w2 + 1) * 128], pAV[:, b2, 0:128]),
                    waits=[(TT, av_tt)])

            for w4 in range(4 if P4PART >= 2 else 0):
                gw = mc * 4 + w4
                mvar = mask0 if gw == 0 else maskm
                kcol = w4 * 128 if gw > 0 else 128
                kwid = 256 if gw > 0 else 128
                for hp in range(8):
                    pb4 = (w4 * 8 + hp) % 2
                    sc_t = {}
                    for hh in range(2):
                        h = hp * 2 + hh
                        w = [(VV, pSC_free[hh])]
                        if w4 == 0 and hp == 0 and hh == 0:
                            w += [(VV, proj_done_v), (SS, proj_done_s)]
                        sc_t[hh] = p.add("tensor", mm(
                            pSC[:, hh, 0:kwid],
                            qf[64 * (h % 2):64 * (h % 2) + 64, h // 2, w4 * 128:(w4 + 1) * 128],
                            kf[64 * (h % 2):64 * (h % 2) + 64, h // 2, kcol:kcol + kwid],
                            True, True), waits=w, inc=(TT, 1))
                    for hh in range(2):
                        pSC_free[hh] = vop(lambda v, b2=hh, m2=mvar, kw=kwid: v.tensor_tensor(
                            amsk[:, b2, 0:kw], pSC[:, b2, 0:kw], m2[:, 0:kw], ALU.add),
                            waits=[(TT, sc_t[hh])] + ([(SS, prev_exp[hh])] if prev_exp.get(hh) else []))
                    for hh in range(2):
                        vop(lambda v, b2=hh, kw=kwid: v.reduce_max(
                            asml[:, b2, 0:1], amsk[:, b2, 0:kw], axis=AX.X, negate=True))
                    for hh in range(2):
                        vop(lambda v, b2=hh: v.tensor_scalar(
                            asml[:, b2, 1:2], asml[:, b2, 0:1], 0.125, None, ALU.mult))
                    vw = p.val(VV)
                    exp_s = {}
                    for hh in range(2):
                        exp_s[hh] = sop(lambda s, b2=hh, kw=kwid: s.activation(
                            aexp[:, b2, 0:kw], amsk[:, b2, 0:kw], AF.Exp,
                            bias=asml[:, b2, 1:2], scale=cvals[:, 2:3],
                            accum_out=asml[:, b2, 2:3]),
                            waits=[(VV, vw)] + ([(SS, prev_prb[hh])] if prev_prb.get(hh) else []))
                    prev_exp = dict(exp_s)
                    for hh in range(2):
                        vop(lambda v, b2=hh: v.reciprocal(asml[:, b2, 3:4], asml[:, b2, 2:3]),
                            waits=[(SS, exp_s[hh])])
                    vw2 = p.val(VV)
                    prb_s = {}
                    for hh in range(2):
                        prb_s[hh] = sop(lambda s, b2=hh, kw=kwid, pb2=pb4: s.activation(
                            aprb[:, pb2, b2, 0:kw], aexp[:, b2, 0:kw], AF.Identity,
                            bias=0.0, scale=asml[:, b2, 3:4]),
                            waits=[(VV, vw2), (TT, tp_par.get(pb4, 0))])
                    prev_prb = dict(prb_s)
                    if pend is not None:
                        emit_tpav(pend)
                    pend = (w4, gw, kwid, hp, dict(prb_s), pb4)
            if pend is not None:
                emit_tpav(pend)
                pend = None
            win_done_v = p.val(VV) if P4PART >= 2 else p.val(VV)
            for tt4 in range(4 if P4PART >= 3 else 0):
                for half in range(2):
                    j = tt4 * 2 + half
                    bb = j % 2
                    for icc in range(8):
                        w = []
                        if j == 0 and icc == 0:
                            w = [(VV, win_done_v)]
                        if icc == 0 and j >= 2:
                            w.append((VV, oev[j - 2]))
                        p.add("tensor", mm(pQK[:, bb, :], apre[:, icc, tt4 * 128:(tt4 + 1) * 128],
                                           wat[:, 3, icc, half * 512:(half + 1) * 512],
                                           icc == 0, icc == 7),
                              waits=w, inc=((TT, 1) if icc == 7 else None))
                    if j == 0:
                        oev = {}
                    oev[j] = vop(lambda v, b2=bb, h2=half: v.tensor_tensor(
                        aout1[:, b2, :], pQK[:, b2, :], bob[:, h2 * 512:(h2 + 1) * 512], ALU.add),
                        waits=[(TT, p.val(TT))] + ([(ST, ost_free[bb])] if bb in ost_free else []))
                    ost_free[bb] = dma("sync", d_attn[t0 + tt4 * 128:t0 + (tt4 + 1) * 128,
                                                      half * 512:(half + 1) * 512],
                                       aout1[:, bb, :], waits=[(VV, oev[j])], s=ST)
            mc_ttdone[mc] = p.val(TT)
            mc_ssdone = p.val(SS)
            mc_vvdone = p.val(VV)
        p4_done = p.val(VV)
        p4_tt = p.val(TT)
        p4_st = p.val(ST)
        es4.close()
        if debug >= 1 and NMC == 8:
            o = dbgten("g_attn", [S, D])
            p.add("sync", lambda e, oo=o: e.dma_start(out=oo[:, :], in_=d_attn[:, :]),
                  waits=[(ST, p4_st)], inc=(ST, 16))
            o = dbgten("g_xn", [S, D], BF16)
            p.add("sync", lambda e, oo=o: e.dma_start(out=oo[:, :], in_=d_xn[:, :]),
                  waits=[(ST, p4_st)], inc=(ST, 16))


        if MAXPH >= 5:
            xn5 = sba("xn5", [128, 2, D], BF16, RC)
            xnb5 = sba("xnb5", [128, 4, 128], BF16, RC + 8192)
            mf5 = sba("mf5", [128, 2, 512], BF16, RC + 12288)
            mo5 = sba("mo5", [128, 2, 2, 512], BF16, RC + 16384)
            me5 = sba("me5", [128, 4, 512], F32, RC + 20480)
            cvb5 = sba("cvb5", [128, 2, 2], F32, RC + 28672)
            stg5 = sba("stg5", [128, 2, 8, 64], BF16, RC + 28736)
            sto5 = sba("sto5", [128, 2, 4, 128], BF16, RC + 30784)
            mvb5 = sba("mvb5", [128, 4, 512], BF16, RC + 34880)
            es5 = ExitStack()
            pF5 = es5.enter_context(nc.psum_tensor("pF5", [128, 2, 512], F32))
            pO5 = es5.enter_context(nc.psum_tensor("pO5", [128, 2, 512], F32))
            d_mov = dt("d_mov", [NTILE, 128, 1024], BF16, kind="Internal").ap()
            mb_done = {}
            fl_st = {}
            fl_ev = {}
            for Ti in range(NTILE):
                b = Ti % 2
                w = [(ST, p4_st), (VV, p4_done)] if Ti == 0 else []
                if Ti >= 2:
                    w.append((VV, mb_done[Ti - 2]))
                t5_ld = dma("sync", xn5[:, b, :], d_xn[Ti * 128:(Ti + 1) * 128, :], waits=w)
                vv_w = [(VV, p4_done)] if Ti == 0 else []
                for k in range(2):
                    vop(lambda v, kk=k, tt_=Ti: v.tensor_scalar(
                        me5[:, kk, :], io512[:, :], colf[:, kk, tt_:tt_ + 1], None, ALU.is_equal),
                        waits=(vv_w if k == 0 else ()))
                for k in range(2):
                    vop(lambda v, kk=k, tt_=Ti: v.tensor_scalar(
                        cvb5[:, kk, 0:1], colov[:, kk, tt_:tt_ + 1], -512.0, None, ALU.add))
                vop(lambda v, bb=b: v.tensor_tensor(mf5[:, bb, :], me5[:, 0, :], me5[:, 1, :], ALU.add))
                for k in range(2):
                    vop(lambda v, kk=k, tt_=Ti: v.tensor_scalar(
                        me5[:, kk, :], io512[:, :], colov[:, kk, tt_:tt_ + 1], None, ALU.is_equal))
                for k in range(2):
                    vop(lambda v, kk=k: v.tensor_scalar(
                        me5[:, 2 + kk, :], io512[:, :], cvb5[:, kk, 0:1], None, ALU.is_equal))
                vop(lambda v, bb=b: v.tensor_tensor(mo5[:, bb, 0, :], me5[:, 0, :], me5[:, 1, :], ALU.add))
                mb_done[Ti] = vop(lambda v, bb=b: v.tensor_tensor(mo5[:, bb, 1, :], me5[:, 2, :], me5[:, 3, :], ALU.add))
                dma("sync", d_mov[Ti, :, :], mo5[:, b, :, :].rearrange("p a b -> p (a b)"),
                    waits=[(VV, mb_done[Ti])], s=ST)
                for ic in range(8):
                    bb2 = ic % 2
                    gkey = Ti * 8 + ic
                    wfl = [(LD, t5_ld), (VV, mb_done[Ti])] if ic == 0 else []
                    if gkey >= 2:
                        wfl = list(wfl) + [(VV, fl_ev[gkey - 2])]
                    p.add("tensor", mm(pF5[:, bb2, :], xn5[:, b, ic * 128:(ic + 1) * 128], mf5[:, b, :],
                                       True, True), waits=wfl, inc=(TT, 1))
                    wv = [(TT, p.val(TT))]
                    if gkey >= 2:
                        wv.append((ST, fl_st[gkey - 2]))
                    fl_ev[gkey] = vop(lambda v, b3=bb2: v.tensor_copy(
                        stg5[:, b3, :, :].rearrange("p a b -> p (a b)"), pF5[:, b3, :]), waits=wv)
                    fl_st[gkey] = dma("sync",
                                      d_buf[:, ic, :, Ti * W:(Ti + 1) * W].rearrange("e p j -> p e j"),
                                      stg5[:, bb2, :, :], waits=[(VV, fl_ev[gkey])], s=ST)
            passA_st = p.val(ST)
            passA_vv = p.val(VV)
            ov_ev = {}
            ov_st = {}
            ldB = {}
            for g in range(16):
                ic, half = g // 2, g % 2
                hb = g % 2
                for Ti in range(NTILE):
                    tb = Ti % 4
                    wl = [(ST, passA_st)] if (g == 0 and Ti == 0) else []
                    if (g * NTILE + Ti) >= 4:
                        wl.append((TT, ldB[(g * NTILE + Ti) - 4]))
                    l1 = dma("sync", xnb5[:, tb, :], d_xn[Ti * 128:(Ti + 1) * 128, ic * 128:(ic + 1) * 128],
                             waits=wl)
                    l2 = dma("sync", mvb5[:, tb, :], d_mov[Ti, :, half * 512:(half + 1) * 512])
                    w3 = [(LD, l2)]
                    if Ti == 0 and g >= 2:
                        w3.append((VV, ov_ev[g - 2]))
                    ldB[g * NTILE + Ti] = p.add("tensor", mm(pO5[:, hb, :], xnb5[:, tb, :], mvb5[:, tb, :],
                                                             Ti == 0, Ti == NTILE - 1),
                                                waits=w3, inc=(TT, 1))
                wv = [(TT, p.val(TT))]
                if g >= 2:
                    wv.append((ST, ov_st[g - 2]))
                ov_ev[g] = vop(lambda v, h3=hb: v.tensor_copy(
                    sto5[:, h3, :, :].rearrange("p a b -> p (a b)"), pO5[:, h3, :]), waits=wv)
                ov_st[g] = dma("sync",
                               d_buf[half * 4:(half + 1) * 4, ic, :, NTILE * W:SLOT_E].rearrange("e p j -> p e j"),
                               sto5[:, hb, :, :], waits=[(VV, ov_ev[g])], s=ST)
            p5_st = p.val(ST)
            p5_tt = p.val(TT)
            es5.close()

        if MAXPH >= 6:
            STS = [384, 384, 384, 384, 384, 256]
            NST = len(STS)
            SOFFS = [sum(STS[:i]) for i in range(NST)]
            TAIL = RD + 65536
            assert TAIL + 6144 <= 229344, TAIL
            w1r = sba("w1r", [128, 8, F4], BF16, RB)
            w2r = sba("w2r", [128, 32, D], BF16, RD)
            h6 = sba("h6", [128, 32, 384], BF16, RC)
            bst6 = sba("bst6", [128, 2, 8, 384], BF16, RC + 24576)
            ysb = sba("ysb", [128, 2, D], BF16, RC + 46080)
            g16 = sba("g16", [128, 4, 384], F32, TAIL)
            es6 = ExitStack()
            pH6 = es6.enter_context(nc.psum_tensor("pH6", [128, 4, 512], F32))
            pY6 = es6.enter_context(nc.psum_tensor("pY6", [128, 2, 1024], F32))
            w1_ldg = {}
            w2_ldg = {}
            w1_lds = {}
            w2_lds = {}
            pcast = {}
            for e9 in range(2):
                dma("gpsimd", d_w1b[e9, :, :], w1[e9, :, :], s=LDG,
                    waits=([(TT, p4_tt)] if e9 == 0 else ()))
                dma("gpsimd", d_w2b[e9, :, :], w2[e9, :, :], s=LDG)
                pcast[e9] = p.val(LDG)
            for icc in range(8):
                dma("gpsimd", w1r[:, icc, :],
                    d_w1b[0, icc * 128:(icc + 1) * 128, :],
                    s=LDG, waits=([(VV, p4_done)] if icc == 0 else ()))
            w1_ldg[0] = p.val(LDG)
            for fg in range(8):
                dma("gpsimd", w2r[:, fg * 4:(fg + 1) * 4, :],
                    d_w2b[0, fg * 512:(fg + 1) * 512, :].rearrange("(c p) o -> p c o", p=128),
                    s=LDG, waits=([(VV, p2_done)] if fg == 0 else ()))
            w2_ldg[0] = p.val(LDG)
            scg = 0
            gfi = 0
            nq = na = nh = 0
            ye_ev = {}
            ysb_st = {}
            h_done = {}
            bst_ldg = {}
            bst_used = {}
            ye_last_mm = {-1: 0}
            gmeta = {}
            sq_v = {}
            q6_v = {}
            ar_v = {}
            sg_v = {}

            def gelu_sq(g):
                hb, w5, tt, _ = gmeta[g]
                sq_v[g] = sop(lambda s, h3=hb, w6=w5: s.activation(
                    g16[:, h3, 0:w6], pH6[:, h3, 0:w6], AF.Square), waits=[(TT, tt)])

            def gelu_q6(g):
                hb, w5, _, _ = gmeta[g]
                q6_v[g] = gop(lambda gg, h3=hb, w6=w5: gg.tensor_scalar(
                    g16[:, h3, 0:w6], g16[:, h3, 0:w6], GELU_C, 1.0, ALU.mult, ALU.add),
                    waits=[(SS, sq_v[g])])

            def gelu_ar(g):
                hb, w5, _, _ = gmeta[g]
                ar_v[g] = vop(lambda v, h3=hb, w6=w5: v.tensor_tensor(
                    g16[:, h3, 0:w6], g16[:, h3, 0:w6], pH6[:, h3, 0:w6], ALU.mult),
                    waits=[(PP, q6_v[g])])

            def gelu_sg(g):
                hb, w5, _, _ = gmeta[g]
                sg_v[g] = sop(lambda s, h3=hb, w6=w5: s.activation(
                    g16[:, h3, 0:w6], g16[:, h3, 0:w6], AF.Sigmoid, bias=0.0, scale=cvals[:, 5:6]),
                    waits=[(VV, ar_v[g])])

            def gelu_hev(g):
                hb, w5, _, f5 = gmeta[g]
                h_done[g] = vop(lambda v, h3=hb, f6=f5, w6=w5: v.tensor_tensor(
                    h6[:, f6, 0:w6], g16[:, h3, 0:w6], pH6[:, h3, 0:w6], ALU.mult),
                    waits=[(SS, sg_v[g])])

            def gelu_pump():
                nonlocal nq, na, nh
                if nq <= gfi - 1:
                    gelu_q6(nq)
                    nq += 1
                if na <= gfi - 2:
                    gelu_ar(na)
                    gelu_sg(na)
                    na += 1
                if nh <= gfi - 3:
                    gelu_hev(nh)
                    nh += 1

            def gelu_flush():
                nonlocal nq, na, nh
                while nq <= gfi:
                    gelu_q6(nq)
                    nq += 1
                while na <= gfi:
                    gelu_ar(na)
                    gelu_sg(na)
                    na += 1
                while nh <= gfi:
                    gelu_hev(nh)
                    nh += 1

            for e_i in range(E):
                lv = dma("sync", rowst[:, :], b2row[:, e_i * D:(e_i + 1) * D],
                         waits=[(TT, ye_last_mm[e_i - 1])] + ([(ST, p5_st)] if e_i == 0 else []))
                for sti in range(NST):
                    stw = STS[sti]
                    soff = SOFFS[sti]
                    sgl = e_i * NST + sti
                    sb6 = sgl % 2
                    wb = [(ST, p5_st)] if sgl == 0 else []
                    if bst_used.get(sb6):
                        wb.append((TT, bst_used[sb6]))
                    bst_ldg[sgl] = dma(
                        "gpsimd", bst6[:, sb6, :, 0:stw],
                        d_buf.rearrange("e c p j -> p e c j")[:, e_i, :, soff:soff + stw],
                        s=LDG, waits=wb)
                    if e_i + 2 < E and sti < 4:
                        dma("gpsimd", d_w1b[e_i + 2, sti * 256:(sti + 1) * 256, :],
                            w1[e_i + 2, sti * 256:(sti + 1) * 256, :], s=LDG)
                        dma("gpsimd", d_w2b[e_i + 2, sti * 1024:(sti + 1) * 1024, :],
                            w2[e_i + 2, sti * 1024:(sti + 1) * 1024, :], s=LDG)
                        if sti == 3:
                            pcast[e_i + 2] = p.val(LDG)
                    lst_f4tt = {}
                    for f4 in range(32):
                        hb = gfi % 4
                        for icc in range(8):
                            w4_ = []
                            if icc == 0:
                                w4_ = [(LDG, w1_ldg[e_i]), (LDG, bst_ldg[sgl])]
                                if w1_lds.get(e_i):
                                    w4_.append((LD, w1_lds[e_i]))
                                if h_done.get(gfi - 4):
                                    w4_.append((VV, h_done[gfi - 4]))
                            p.add("tensor", mm(pH6[:, hb, 0:stw], w1r[:, icc, f4 * 128:(f4 + 1) * 128],
                                               bst6[:, sb6, icc, 0:stw], icc == 0, icc == 7),
                                  waits=w4_, inc=((TT, 1) if icc == 7 else None))
                        h_mm = p.val(TT)
                        lst_f4tt[f4] = h_mm
                        gmeta[gfi] = (hb, stw, h_mm, f4)
                        gelu_sq(gfi)
                        gelu_pump()
                        gfi += 1
                    gfi -= 1
                    gelu_flush()
                    gfi += 1
                    h_all = p.val(VV)
                    bst_used[sb6] = p.val(TT)
                    if sti == NST - 1 and e_i + 1 < E:
                        for fh in range(2):
                            for icc in range(8):
                                dma("gpsimd" if icc == 0 else "sync",
                                    w1r[:, icc, fh * 2048:(fh + 1) * 2048],
                                    d_w1b[e_i + 1, icc * 128:(icc + 1) * 128, fh * 2048:(fh + 1) * 2048],
                                    s=(LDG if icc == 0 else LD),
                                    waits=(([(TT, lst_f4tt[16 * fh + 15])] +
                                            ([(LDG, pcast[e_i + 1])] if fh == 0 else []))
                                           if icc <= 1 else ()))
                        w1_ldg[e_i + 1] = p.val(LDG)
                        w1_lds[e_i + 1] = p.val(LD)
                    nsc = stw // 128
                    for sc in range(nsc):
                        yb = scg % 2
                        for half in range(2):
                            wy = [(LD, lv)]
                            if half == 0:
                                wy = [(VV, h_all), (LDG, w2_ldg[e_i]), (LD, lv)]
                                if w2_lds.get(e_i):
                                    wy.append((LD, w2_lds[e_i]))
                                if ye_ev.get(scg - 2):
                                    wy.append((VV, ye_ev[scg - 2]))
                            p.add("tensor", mm(pY6[:, yb, half * 512:(half + 1) * 512],
                                               ones1f[0:1, :],
                                               rowst[0:1, half * 512:(half + 1) * 512],
                                               True, False), waits=wy, inc=None)
                            for f4 in range(32):
                                p.add("tensor", mm(pY6[:, yb, half * 512:(half + 1) * 512],
                                                   h6[:, f4, sc * 128:(sc + 1) * 128],
                                                   w2r[:, f4, half * 512:(half + 1) * 512],
                                                   False, f4 == 31),
                                      inc=((TT, 1) if (f4 == 31 and half == 1) else None))
                        ye_mm = p.val(TT)
                        for half in range(2):
                            wv = [(TT, ye_mm)] if half == 0 else []
                            if half == 0 and ysb_st.get(scg - 2):
                                wv.append((ST, ysb_st[scg - 2]))
                            vop(lambda v, y3=yb, h3=half: v.tensor_copy(
                                ysb[:, y3, h3 * 512:(h3 + 1) * 512],
                                pY6[:, y3, h3 * 512:(h3 + 1) * 512]), waits=wv)
                        ye_ev[scg] = p.val(VV)
                        gsc = soff // 128 + sc
                        ysb_st[scg] = dma("sync", d_ye[e_i, gsc, :, :], ysb[:, yb, :],
                                          waits=[(VV, ye_ev[scg])], s=ST)
                        scg += 1
                ye_last_mm[e_i] = p.val(TT)
                if e_i + 1 < E:
                    for fg in range(8):
                        dma("gpsimd" if fg == 0 else "sync",
                            w2r[:, fg * 4:(fg + 1) * 4, :],
                            d_w2b[e_i + 1, fg * 512:(fg + 1) * 512, :].rearrange("(c p) o -> p c o", p=128),
                            s=(LDG if fg == 0 else LD),
                            waits=(([(TT, ye_last_mm[e_i]), (LDG, pcast[e_i + 1])] if fg <= 1 else ())))
                    w2_ldg[e_i + 1] = p.val(LDG)
                    w2_lds[e_i + 1] = p.val(LD)
            p6_st = p.val(ST)
            p6_vv = p.val(VV)
            p6_tt = p.val(TT)
            es6.close()

        if MAXPH >= 7:
            yeov = sba("yeov", [128, 8, D], BF16, RD)
            ystk = sba("ystk", [128, 2, 4, D], BF16, RD + 16384)
            mg7 = sba("mg7", [128, 512], BF16, RC)
            mgo7 = sba("mgo7", [128, 2, 512], BF16, RC + 1024)
            me7 = sba("me7", [128, 4, 512], F32, RC + 3072)
            mgt7 = sba("mgt7", [128, 4, 128], BF16, RC + 11264)
            mgo7t = sba("mgo7t", [128, 8, 128], BF16, RC + 12288)
            cvb7 = sba("cvb7", [128, 2, 1], F32, RC + 14336)
            xld7 = sba("xld7", [128, 2, D], F32, RC + 16384)
            atl7 = sba("atl7", [128, 2, D], F32, RC + 24576)
            out7 = sba("out7", [128, 2, D], F32, RC + 32768)
            es7 = ExitStack()
            pMG = es7.enter_context(nc.psum_tensor("pMG", [128, 2, 1024], BF16))
            pFF = es7.enter_context(nc.psum_tensor("pFF", [128, 2, 1024], F32))
            if P7PART >= 1:
                ld_ov = dma("sync", yeov[:, :, :],
                            d_ye[:, NSC - 1, :, :].rearrange("e p o -> p e o"), waits=[(ST, p6_st)])
            out_stv = {}
            cmb_ev = {}
            mgt_ev = {}
            ld_yp = {}
            ff_mm = {}
            tpz_d = {}
            la_d = {}

            def emit_cmb(j):
                bj = j % 2
                cw = [(TT, ff_mm.get(j, 0)), (LD, la_d[j])]
                if out_stv.get(j - 2):
                    cw.append((ST, out_stv[j - 2]))
                for half in range(2):
                    if P7PART >= 4:
                        vop(lambda v, b3=bj, h3=half: v.tensor_tensor(
                            out7[:, b3, h3 * 512:(h3 + 1) * 512], pFF[:, b3, h3 * 512:(h3 + 1) * 512],
                            xld7[:, b3, h3 * 512:(h3 + 1) * 512], ALU.add), waits=(cw if half == 0 else ()))
                    else:
                        vop(lambda v, b3=bj, h3=half: v.tensor_copy(
                            out7[:, b3, h3 * 512:(h3 + 1) * 512],
                            xld7[:, b3, h3 * 512:(h3 + 1) * 512]), waits=(cw if half == 0 else ()))
                for half in range(2):
                    vop(lambda v, b3=bj, h3=half: v.tensor_tensor(
                        out7[:, b3, h3 * 512:(h3 + 1) * 512], out7[:, b3, h3 * 512:(h3 + 1) * 512],
                        atl7[:, b3, h3 * 512:(h3 + 1) * 512], ALU.add))
                cmb_ev[j] = p.val(VV)
                out_stv[j] = dma("sync", y[j * 128:(j + 1) * 128, :], out7[:, bj, :],
                                 waits=[(VV, cmb_ev[j])], s=ST)
            for Ti in range(NTILE):
                b = Ti % 2
                ch = (Ti * W) // 128
                pb = Ti % 2
                ro = 64 * (Ti % 2)
                ypw = [(ST, p6_st)] if Ti == 0 else []
                if ff_mm.get(Ti - 2):
                    ypw.append((TT, ff_mm[Ti - 2]))
                for j in range(4 if P7PART >= 1 else 0):
                    for s2 in range(2):
                        ld_yp[Ti] = dma("sync", ystk[64 * s2:64 * s2 + 64, b, j, :],
                                        d_ye[2 * j + s2, ch, ro:ro + 64, :],
                                        waits=(ypw if (j == 0 and s2 == 0) else ()))
                vvw = [(VV, p6_vv), (TT, p6_tt)] if Ti == 0 else []
                if P7PART >= 2:
                    if mgt_ev.get(Ti - 1):
                        vvw = list(vvw) + [(TT, tpz_d.get(Ti - 1, 0))]
                    for k in range(2):
                        vop(lambda v, kk=k, tt_=Ti: v.tensor_scalar(
                            me7[:, kk, :], io512[:, :], colf[:, kk, tt_:tt_ + 1], None, ALU.is_equal),
                            waits=(vvw if k == 0 else ()))
                    for k in range(2):
                        vop(lambda v, kk=k, tt_=Ti: v.tensor_scalar(
                            me7[:, kk, :], me7[:, kk, :], gsel[:, kk, tt_:tt_ + 1], None, ALU.mult))
                    for k in range(2):
                        vop(lambda v, kk=k, tt_=Ti: v.tensor_scalar(
                            cvb7[:, kk, 0:1], colov[:, kk, tt_:tt_ + 1], -512.0, None, ALU.add))
                    v_mg = vop(lambda v: v.tensor_tensor(mg7[:, :], me7[:, 0, :], me7[:, 1, :], ALU.add))
                    for k in range(2):
                        vop(lambda v, kk=k, tt_=Ti: v.tensor_scalar(
                            me7[:, kk, :], io512[:, :], colov[:, kk, tt_:tt_ + 1], None, ALU.is_equal))
                    for k in range(2):
                        vop(lambda v, kk=k: v.tensor_scalar(
                            me7[:, 2 + kk, :], io512[:, :], cvb7[:, kk, 0:1], None, ALU.is_equal))
                    for k in range(2):
                        vop(lambda v, kk=k, tt_=Ti: v.tensor_scalar(
                            me7[:, kk, :], me7[:, kk, :], gsel[:, kk, tt_:tt_ + 1], None, ALU.mult))
                    for k in range(2):
                        vop(lambda v, kk=k, tt_=Ti: v.tensor_scalar(
                            me7[:, 2 + kk, :], me7[:, 2 + kk, :], gsel[:, kk, tt_:tt_ + 1], None, ALU.mult))
                    v_mo1 = vop(lambda v: v.tensor_tensor(mgo7[:, 0, :], me7[:, 0, :], me7[:, 1, :], ALU.add))
                    v_mo2 = vop(lambda v: v.tensor_tensor(mgo7[:, 1, :], me7[:, 2, :], me7[:, 3, :], ALU.add))
                if P7PART >= 3:
                    tw0 = [(VV, v_mo2)]
                    if mgt_ev.get(Ti - 1):
                        tw0.append((VV, mgt_ev[Ti - 1]))
                    for j in range(4):
                        p.add("tensor", lambda t, jj=j: t.transpose(
                            pMG[:, 0, jj * 128:(jj + 1) * 128], mg7[:, jj * 128:(jj + 1) * 128], identb[:, :]),
                            waits=(tw0 if j == 0 else ()), inc=((TT, 1) if j == 3 else None))
                    tpf = p.val(TT)
                    for hh in range(2):
                        for j in range(4):
                            p.add("tensor", lambda t, h3=hh, jj=j: t.transpose(
                                pMG[:, 1, (h3 * 4 + jj) * 128:(h3 * 4 + jj + 1) * 128],
                                mgo7[:, h3, jj * 128:(jj + 1) * 128], identb[:, :]),
                                inc=((TT, 1) if (hh == 1 and j == 3) else None))
                    tpz = p.val(TT)
                    tpz_d[Ti] = tpz
                    ev1 = vop(lambda v: v.tensor_copy(mgt7.rearrange("p a b -> p (a b)"), pMG[:, 0, 0:512]),
                              waits=[(TT, tpf)])
                    mgt_ev[Ti] = vop(lambda v: v.tensor_copy(mgo7t.rearrange("p a b -> p (a b)"), pMG[:, 1, :]),
                                     waits=[(TT, tpz)])
                for half in range(2 if P7PART >= 4 else 0):
                    first = True
                    for j in range(4):
                        w = []
                        if first:
                            w = [(VV, mgt_ev[Ti]), (LD, ld_yp[Ti]), (LD, ld_ov)]
                            if cmb_ev.get(Ti - 2):
                                w.append((VV, cmb_ev[Ti - 2]))
                        p.add("tensor", mm(pFF[:, pb, half * 512:(half + 1) * 512],
                                           mgt7[:, j, :],
                                           ystk[:, b, j, half * 512:(half + 1) * 512],
                                           first, False), waits=w, inc=None)
                        first = False
                    for e_i in range(E):
                        p.add("tensor", mm(pFF[:, pb, half * 512:(half + 1) * 512],
                                           mgo7t[:, e_i, :],
                                           yeov[:, e_i, half * 512:(half + 1) * 512],
                                           False, e_i == E - 1),
                              inc=((TT, 1) if e_i == E - 1 else None))
                ff_mm[Ti] = p.val(TT)
                lxw = [(ST, p6_st), (VV, p6_vv), (TT, p6_tt)] if Ti == 0 else []
                if cmb_ev.get(Ti - 2):
                    lxw = list(lxw) + [(VV, cmb_ev[Ti - 2])]
                dma("sync", xld7[:, b, :], x_sh[Ti * 128:(Ti + 1) * 128, :], waits=lxw)
                la_d[Ti] = dma("sync", atl7[:, b, :], d_attn[Ti * 128:(Ti + 1) * 128, :])
                if Ti > 0:
                    emit_cmb(Ti - 1)
            emit_cmb(NTILE - 1)
            p7_st = p.val(ST)

        p.add("sync", lambda e: e.nop(),
              waits=[(ST, p.val(ST)), (VV, p.val(VV)), (SS, p.val(SS)), (TT, p.val(TT)),
                     (LDG, p.val(LDG)), (LD, p.val(LD)), (PP, p.val(PP))])
        block = es.enter_context(nc.Block())
        p.emit(block)
    return nc, dbg


def host_inputs(inputs, core):
    x = np.asarray(inputs["L0_attn_x"]).reshape(T, D)
    xr = np.concatenate([x[core * S:], x[:core * S]], axis=0)
    io512 = np.tile(np.arange(512, dtype=np.float32), (128, 1))
    wrap = np.zeros((1, NCOL), np.float32)
    nwrap = (core * S) // 128
    if nwrap:
        wrap[0, NCOL - nwrap:] = 1.0
    return {
        "x_sh": np.ascontiguousarray(x[core * S:(core + 1) * S]),
        "xfm": np.ascontiguousarray(xr.T),
        "wq": np.ascontiguousarray(inputs["L0_attn_Wq"]), "wk": np.ascontiguousarray(inputs["L0_attn_Wk"]),
        "wv": np.ascontiguousarray(inputs["L0_attn_Wv"]), "wo": np.ascontiguousarray(inputs["L0_attn_Wo"]),
        "bqr": np.ascontiguousarray(np.asarray(inputs["L0_attn_bq"]).reshape(8, 128).T),
        "bkr": np.ascontiguousarray(np.asarray(inputs["L0_attn_bk"]).reshape(8, 128).T),
        "bvrow": np.ascontiguousarray(np.asarray(inputs["L0_attn_bv"]).reshape(1, D)),
        "borow": np.ascontiguousarray(np.asarray(inputs["L0_attn_bo"]).reshape(1, D)),
        "wgi": np.ascontiguousarray(inputs["L0_ffn_Wg"]),
        "bgc": np.ascontiguousarray(np.asarray(inputs["L0_ffn_bg"]).reshape(8, 1)),
        "w1": np.ascontiguousarray(inputs["L0_ffn_W1"]),
        "b1r": np.ascontiguousarray(np.asarray(inputs["L0_ffn_b1"]).reshape(8, 32, 128).transpose(2, 1, 0)),
        "w2": np.ascontiguousarray(inputs["L0_ffn_W2"]),
        "b2row": np.ascontiguousarray(np.asarray(inputs["L0_ffn_b2"]).reshape(1, E * D)),
        "c_lt": np.triu(np.ones((128, 128), np.float32), 1),
        "c_io512": io512,
        "c_io512b": io512 + 512.0,
        "c_io8": np.tile(np.arange(8, dtype=np.float32), (128, 1)),
        "c_identf": np.eye(128, dtype=np.float32),
        "c_identb": np.eye(128).astype(ml_dtypes.bfloat16),
        "c_mask": np.where(
            (np.arange(256)[None, :] < 128) | ((np.arange(256)[None, :] - 128) <= np.arange(128)[:, None]),
            0.0, -1e30).astype(np.float32),
        "c_mask0": np.where(np.arange(128)[None, :] <= np.arange(128)[:, None], 0.0, -1e30).astype(np.float32),
        "wrapm": wrap,
        "c_vals": np.tile(np.array([1e-5, -0.5, 0.125, 0.5, GELU_C, GELU_S, 1.0, 0.0], np.float32), (128, 1)),
    }


def kernel(**inputs):
    nc, _ = build(debug=0)
    ims = [host_inputs(inputs, c) for c in range(NCORE)]
    res = run_bass_kernel_spmd(nc, ims, core_ids=list(range(NCORE)))
    out = np.stack([np.asarray(res.results[c]["y"]) for c in range(NCORE)], axis=0)
    return out.astype(np.float32)
